# version 21
# baseline (speedup 1.0000x reference)
import numpy as np
import concourse.bass as bass
import concourse.mybir as mybir
from concourse.bass_utils import run_bass_kernel_spmd

F32 = mybir.dt.float32
BF16 = mybir.dt.bfloat16
AF = mybir.ActivationFunctionType
ALU = mybir.AluOpType
AX = mybir.AxisListType

B_TOT, S, DM = 32, 2048, 1024
DBG_C0_INLOOP = False
DBG_C0_BIAS = True
DBG_NO_C0 = True
DBG_P_START = 1
DBG_DUMP_PC = False
H, DH = 16, 64
NCORE = 8
SEQ_PER_CORE = B_TOT // NCORE
NBLK = S // 128
KC = DM // 128
NE = 8
EPS = 1e-5

R_KT = 6
R_V = 6
R_QT = 4
R_EKM = 10


class Sem:
    __slots__ = ("name", "count", "handle")

    def __init__(self, name):
        self.name = name
        self.count = 0
        self.handle = None


def tk(*parts):
    best = {}
    for p in parts:
        if p is None:
            continue
        if isinstance(p, tuple):
            p = [p]
        for s, v in p:
            if best.get(s, 0) < v:
                best[s] = v
    return list(best.items())


class Sched:
    def __init__(self):
        self.steps = []
        self.sems = {}
        self.n_drains = 0

    def sem(self, name):
        if name not in self.sems:
            self.sems[name] = Sem(name)
        return self.sems[name]

    def emit(self, eng, fn, waits=None, inc=None, n=1, reads=(), writes=()):
        w = tk(waits)
        t = None
        if inc is not None:
            inc.count += n
            t = (inc, inc.count)
        self.steps.append((eng, fn, w, (inc, n) if inc else None,
                           tuple(reads), tuple(writes)))
        return t

    def replay_engine(self, eng_name, eng):
        last_wait = {}
        wpos = {}
        rpos = {}
        idx = 0
        cleared_below = -10**9
        WINDOW = 10
        for en, fn, waits, inc, reads, writes in self.steps:
            if en != eng_name:
                continue
            if eng_name in ("vector", "scalar"):
                idx += 1
                hz = max(
                    [wpos.get(r, -10**9) for r in reads]
                    + [max(wpos.get(w, -10**9), rpos.get(w, -10**9)) for w in writes]
                    + [-10**9])
                if hz >= cleared_below and idx - hz <= WINDOW:
                    eng.drain()
                    self.n_drains += 1
                    cleared_below = idx
                for w in writes:
                    wpos[w] = idx
                for r in reads:
                    rpos[r] = idx
            for sem, val in waits:
                if last_wait.get(sem.name, 0) >= val:
                    continue
                eng.wait_ge(sem.handle, val)
                last_wait[sem.name] = val
            inst = fn(eng)
            if inc is not None:
                sem, nn = inc
                inst.then_inc(sem.handle, nn)


def build_nc(n_seq=SEQ_PER_CORE, n_blk=NBLK, phases=("A", "B", "C"), taps=(), b_stop=3):
    nc = bass.Bass("TRN2", target_bir_lowering=False)
    NT = n_seq * n_blk * 128

    x_d = nc.dram_tensor("x", [NT, DM], F32, kind="ExternalInput").ap()
    y_d = nc.dram_tensor("y", [NT, DM], F32, kind="ExternalOutput").ap()
    wqkv_d = nc.dram_tensor("wqkv", [128, 24, KC, 128], F32, kind="ExternalInput").ap()
    wproj_d = nc.dram_tensor("wproj", [128, KC, DM], F32, kind="ExternalInput").ap()
    bproj_d = nc.dram_tensor("bproj", [1, DM], F32, kind="ExternalInput").ap()
    rw_d = nc.dram_tensor("rw", [128, KC, NE], F32, kind="ExternalInput").ap()
    rb_d = nc.dram_tensor("rb", [1, NE], F32, kind="ExternalInput").ap()
    ew_d = nc.dram_tensor("ew", [NE, 128, KC * DM], F32, kind="ExternalInput").ap()
    eb_d = nc.dram_tensor("eb", [NE, DM], F32, kind="ExternalInput").ap()
    ident_d = nc.dram_tensor("identd", [128, 128], BF16, kind="ExternalInput").ap()
    wqc_d = nc.dram_tensor("wqc", [128, 24, KC, 128], BF16,
                           kind="ExternalOutput").ap()
    mlo_d = nc.dram_tensor("mlod", [128, 128], BF16, kind="ExternalInput").ap()
    mhi_d = nc.dram_tensor("mhid", [128, 128], BF16, kind="ExternalInput").ap()
    tap_d = {nm: nc.dram_tensor("tap_" + nm, shp, F32, kind="ExternalOutput").ap()
             for nm, shp in taps}

    sch = Sched()
    PEs = sch.sem("pe")
    DVs = sch.sem("dv")
    ACs = sch.sem("ac")
    NLANE = 4
    DG_l = [sch.sem(f"dg{i}") for i in range(NLANE)]
    DS_l = [sch.sem(f"ds{i}") for i in range(NLANE)]
    dma_rr = {"g": 0, "s": 0}

    ctx = []
    sbuf_bytes = [0]

    def sb(name, shape, dtype):
        t = nc.sbuf_tensor(name, shape, dtype)
        h = t.__enter__()
        ctx.append(t)
        per_part = int(np.prod(shape[1:])) * (2 if dtype == BF16 else 4)
        sbuf_bytes[0] += per_part
        return h

    x_sb = sb("x_sb", [128, n_blk, DM], BF16)
    xnT = sb("xnT", [128, KC, n_blk, 128], BF16)
    xn_st = sb("xn_st", [128, 2, DM], BF16)
    wexp = sb("wexp", [128, 2, KC, DM], BF16)
    wqkv_sb = sb("wqkv_sb", [128, 2, 4, KC, 128], BF16)
    wproj_sb = sb("wproj_sb", [128, KC, DM], BF16)
    rw_sb = sb("rw_sb", [128, KC, NE], BF16)
    eb_sb = sb("eb_sb", [NE, DM], BF16)
    rb_sb = sb("rb_sb", [128, NE], F32)
    rb_row = sb("rb_row", [1, NE], F32)
    bproj_sb = sb("bproj_sb", [128, DM], BF16)
    onesr = sb("onesr", [1, 128], F32)
    epsc = sb("epsc", [128, 1], F32)
    zc = sb("zc", [128, 1], F32)
    ident = sb("ident", [128, 128], BF16)
    mlo = sb("mlo", [128, 128], BF16)
    mhi = sb("mhi", [128, 128], BF16)
    qT = sb("qT", [128, R_QT, KC, 128], BF16)
    kT = sb("kT", [128, R_KT, KC, 128], BF16)
    vtok = sb("vtok", [128, R_V, H, 65], BF16)
    ekm = sb("ekm", [128, R_EKM, 384], BF16)
    vfeat = sb("vfeat", [128, 2, 256], BF16)
    anorm = sb("anorm", [128, 2, DM], BF16)
    aT = sb("aT", [128, 2, KC, 128], BF16)
    out_st = sb("out_st", [128, 2, DM], F32)
    ffn_tmp = sb("ffn_tmp", [128, 3, 512], BF16)
    mv6 = sb("mv6", [128, 2, 12], F32)
    mva = sb("mva", [128, 2, 2], F32)
    lnv = sb("lnv", [128, 2, 1], F32)
    inv = sb("inv", [128, 2, 1], F32)
    logit = sb("logit", [128, 2, NE], F32)
    mxb = sb("mxb", [128, 2, 1], F32)
    nmx = sb("nmx", [128, 2, 1], F32)
    sume = sb("sume", [128, 2, 1], F32)
    gate = sb("gate", [128, 2, 1], F32)
    probs = sb("probs", [128, 2, NE], F32)
    pcoef = sb("pcoef", [128, n_blk, NE], F32)
    cTt = sb("cTt", [NE, n_blk, 128], BF16)
    invl = sb("invl", [128, 2, H], F32)
    pcb = sb("pcb", [128, n_blk, NE], BF16)


    PS = nc.psum_tensor("PS", [128, 3072], F32)
    PSh = PS.__enter__(); ctx.append(PS)
    PT = nc.psum_tensor("PT", [128, 2048], BF16)
    PTh = PT.__enter__(); ctx.append(PT)

    def bank(i, off=0, width=512):
        assert 0 <= i < 6 and off + width <= 512
        return PSh[:, 512 * i + off: 512 * i + off + width]

    def tbank(i, off=0, width=128):
        assert 0 <= i < 2 and off + width <= 1024
        return PTh[:, 1024 * i + off: 1024 * i + off + width]

    bank_rel = [None] * 6
    tbank_rel = [None] * 2
    xslot_rel = [None] * n_blk
    kt_rel = [None] * R_KT
    v_rel = [None] * R_V
    qt_rel = [None] * R_QT
    ekm_rel = [None] * R_EKM
    wexp_rel = [None, None]
    wq_rel = [None, None]
    outst_rel = [None, None]
    sm_rel = {}

    def _dma(ring, eng, lanes, dst, src, waits):
        i = dma_rr[ring]
        dma_rr[ring] += 1
        lane = lanes[i % NLANE]
        return sch.emit(eng, lambda g, d=dst, s=src: g.dma_start(d, s),
                        tk(waits, (lane, lane.count)), lane, 16)

    def dma_g(dst, src, waits=None, tag=""):
        return _dma("g", "gpsimd", DG_l, dst, src, waits)

    def dma_s(dst, src, waits=None, tag=""):
        return _dma("s", "sync", DS_l, dst, src, waits)

    t_ident = dma_s(ident[:, :], ident_d)
    t_mlo = dma_s(mlo[:, :], mlo_d)
    t_mhi = dma_s(mhi[:, :], mhi_d)
    t_rw = dma_g(rw_sb[:, :, :], rw_d)
    t_eb = dma_g(eb_sb[:, :], eb_d)
    t_rbr = dma_g(rb_row[:, :], rb_d)
    t_bpr = dma_g(out_st[0:1, 0, :], bproj_d)
    t_wproj_box = [None]

    def load_wproj():
        t_wproj_box[0] = dma_g(wproj_sb[:, :, :], wproj_d)
        return t_wproj_box[0]
    t_cst = sch.emit("vector", lambda v: v.memset(onesr[:, :], 1.0), None, DVs, 1,
                     writes=["onesr"])
    t_cst2 = sch.emit("vector", lambda v: v.memset(epsc[:, :], EPS), None, DVs, 1,
                      writes=["epsc"])
    t_cst3 = sch.emit("vector", lambda v: v.memset(zc[:, :], 0.0), None, DVs, 1,
                      writes=["zc"])
    t_vones = sch.emit("vector", lambda v: v.memset(vtok[:, :, :, 64:65], 1.0),
                       None, DVs, 1, writes=["vones"])
    t_b1 = sch.emit("tensor",
                    lambda t: t.matmul(bank(0, 0, NE), onesr[:, :], rb_row[:, :],
                                       start=True, stop=True),
                    tk(t_cst, t_rbr), PEs, 1)
    t_rb = sch.emit("vector", lambda v: v.tensor_copy(rb_sb[:, :], bank(0, 0, NE)),
                    tk(t_b1), DVs, 1, reads=["ps0"], writes=["rb_sb"])
    bank_rel[0] = tk(t_rb)
    t_b2 = sch.emit("tensor",
                    lambda t: t.matmul(bank(1), onesr[:, :], out_st[0:1, 0, 0:512],
                                       start=True, stop=True),
                    tk(t_cst, t_bpr), PEs, 1)
    t_b3 = sch.emit("tensor",
                    lambda t: t.matmul(bank(2), onesr[:, :], out_st[0:1, 0, 512:1024],
                                       start=True, stop=True),
                    tk(t_b2), PEs, 1)
    t_bp1 = sch.emit("vector", lambda v: v.tensor_copy(bproj_sb[:, 0:512], bank(1)),
                     tk(t_b3), DVs, 1, reads=["ps1"], writes=["bproj_sb"])
    t_bp2 = sch.emit("vector", lambda v: v.tensor_copy(bproj_sb[:, 512:1024], bank(2)),
                     tk(t_b3), DVs, 1, reads=["ps2"], writes=["bproj_sb"])
    bank_rel[1] = tk(t_bp1)
    bank_rel[2] = tk(t_bp2)
    t_bp = tk(t_bp1, t_bp2)
    outst_rel[0] = tk(t_b3)


    qc_box = [None]

    def build_qkv_cache():
        qc_store = [None] * 6
        for jb in range(6):
            sl = jb % 2
            tl = dma_g(wqkv_sb[:, sl, :, :, :],
                       wqkv_d[:, 4 * jb:4 * jb + 4, :, :],
                       tk(qc_store[jb - 2] if jb >= 2 else None))
            qc_store[jb] = dma_s(wqc_d[:, 4 * jb:4 * jb + 4, :, :],
                                 wqkv_sb[:, sl, :, :, :], tk(tl))
        qc_box[0] = tk(*qc_store)

    tr_rot = [0]
    trb_rot = [0]

    def transpose_burst(srcs, dst_ap, waits, wkey):
        n = len(srcs)
        bank_i = trb_rot[0] % 2
        trb_rot[0] += 1
        wl = tk(tbank_rel[bank_i], waits, t_ident)
        pt = None
        for j, s_ap in enumerate(srcs):
            pt = sch.emit("tensor",
                          lambda t, ss=s_ap, bb=bank_i, jj=j: t.transpose(
                              PTh[:, 1024 * bb + 128 * jj: 1024 * bb + 128 * (jj + 1)],
                              ss, ident[:, :]),
                          wl if j == 0 else None,
                          PEs if j == n - 1 else None, 1)
        src_ap = PTh[:, 1024 * bank_i: 1024 * bank_i + 128 * n]
        if bank_i == 0:
            ev = sch.emit("vector",
                          lambda v, d=dst_ap, s2=src_ap: v.tensor_copy(d, s2),
                          tk(pt), DVs, 1, reads=["pt"], writes=[wkey])
        else:
            ev = sch.emit("scalar",
                          lambda sc, d=dst_ap, s2=src_ap: sc.copy(d, s2),
                          tk(pt), ACs, 1, reads=["apt"], writes=["a" + wkey])
        tbank_rel[bank_i] = tk(ev)
        return ev

    def transpose_burst_multi(pairs, waits, wkey):
        n = len(pairs)
        bank_i = trb_rot[0] % 2
        trb_rot[0] += 1
        wl = tk(tbank_rel[bank_i], waits, t_ident)
        pt = None
        for j, (s_ap, _) in enumerate(pairs):
            pt = sch.emit("tensor",
                          lambda t, ss=s_ap, bb=bank_i, jj=j: t.transpose(
                              PTh[:, 1024 * bb + 128 * jj: 1024 * bb + 128 * (jj + 1)],
                              ss, ident[:, :]),
                          wl if j == 0 else None,
                          PEs if j == n - 1 else None, 1)
        evs = []
        for j, (_, d_ap) in enumerate(pairs):
            s2 = PTh[:, 1024 * bank_i + 128 * j: 1024 * bank_i + 128 * (j + 1)]
            if bank_i == 0:
                evs.append(sch.emit("vector",
                                    lambda v, d=d_ap, ss=s2: v.tensor_copy(d, ss),
                                    tk(pt), DVs, 1, reads=["pt"],
                                    writes=[f"{wkey}:{j}"]))
            else:
                evs.append(sch.emit("scalar",
                                    lambda sc, d=d_ap, ss=s2: sc.copy(d, ss),
                                    tk(pt), ACs, 1, reads=["apt"],
                                    writes=[f"a{wkey}:{j}"]))
        tbank_rel[bank_i] = tk(*evs)
        return evs

    def transpose128(src_ap, dst_ap, waits, rkey, wkey):
        b = tr_rot[0] % 2
        tr_rot[0] += 1
        pt = sch.emit("tensor",
                      lambda t, s=src_ap, bb=b: t.transpose(tbank(bb), s, ident[:, :]),
                      tk(tbank_rel[b], waits, t_ident), PEs, 1)
        dt_ = sch.emit("vector",
                       lambda v, d=dst_ap, bb=b: v.tensor_copy(d, tbank(bb)),
                       tk(pt), DVs, 1, reads=[rkey], writes=[wkey])
        tbank_rel[b] = tk(dt_)
        return dt_

    def ln_stats(b, xin_ticket):
        pp = b % 2
        xk = f"x:{b}"
        t1 = sch.emit("vector",
                      lambda v: v.bn_stats(mv6[:, pp, 0:6], x_sb[:, b, 0:512]),
                      tk(xin_ticket, sm_rel.get(("mv6", pp))), DVs, 1,
                      reads=[xk + ":0", xk + ":1"], writes=[f"mv6:{pp}"])
        t2 = sch.emit("vector",
                      lambda v: v.bn_stats(mv6[:, pp, 6:12], x_sb[:, b, 512:1024]),
                      None, DVs, 1, reads=[xk + ":0", xk + ":1"], writes=[f"mv6:{pp}"])
        t3 = sch.emit("vector",
                      lambda v: v.bn_aggr(mva[:, pp, :], mv6[:, pp, :]),
                      tk(sm_rel.get(("mva", pp))), DVs, 1,
                      reads=[f"mv6:{pp}"], writes=[f"mva:{pp}"])
        t4 = sch.emit("scalar",
                      lambda sc: sc.activation(lnv[:, pp, :], mva[:, pp, 1:2],
                                               AF.Ln, bias=epsc[:, :], scale=1.0),
                      tk(t3, t_cst2, sm_rel.get(("lnv", pp))), ACs, 1,
                      reads=[f"amva:{pp}"], writes=[f"lnv:{pp}"])
        t5 = sch.emit("scalar",
                      lambda sc: sc.activation(inv[:, pp, :], lnv[:, pp, :],
                                               AF.Exp, bias=zc[:, :], scale=-0.5),
                      tk(t_cst3, sm_rel.get(("inv", pp))), ACs, 1,
                      reads=[f"lnv:{pp}"], writes=[f"inv:{pp}"])
        t6 = sch.emit("vector",
                      lambda v: v.tensor_scalar(xn_st[:, pp, :], x_sb[:, b, :],
                                                mva[:, pp, 0:1], inv[:, pp, :],
                                                op0=ALU.subtract, op1=ALU.mult),
                      tk(t5, sm_rel.get(("xn_st", pp))), DVs, 1,
                      reads=[xk + ":0", xk + ":1", f"mva:{pp}"], writes=[f"xn_st:{pp}"])
        sm_rel[("mva", pp)] = tk(t4, t6)
        sm_rel[("lnv", pp)] = tk(t5)
        sm_rel[("inv", pp)] = tk(t6)
        sm_rel[("mv6", pp)] = tk(t3)
        return t6

    def ln_transposes(b, t6):
        pp = b % 2
        evs = []
        for h4 in range(2):
            evs.append(transpose_burst(
                [xn_st[:, pp, 128 * k:128 * (k + 1)]
                 for k in range(4 * h4, 4 * h4 + 4)],
                xnT[:, 4 * h4:4 * h4 + 4, b, :], tk(t6),
                wkey=f"xnT:{b}:{h4}"))
        sm_rel[("xn_st", pp)] = tk(*evs, (PEs, PEs.count))
        return tk(*evs)

    def ln_block(b, xin_ticket, phase):
        return ln_transposes(b, ln_stats(b, xin_ticket))

    def router_block(b, xnT_t):
        pp = b % 2
        wl = tk(xnT_t, bank_rel[5], t_rw, t_rb, sm_rel.get(("logit", pp)))
        tm = None
        for k in range(KC):
            tm = sch.emit("tensor",
                          lambda t, kk=k: t.matmul(bank(5, 0, NE),
                                                   xnT[:, kk, b, :], rw_sb[:, kk, :],
                                                   start=(kk == 0), stop=(kk == KC - 1)),
                          wl if k == 0 else None, PEs if k == KC - 1 else None, 1)
        t1 = sch.emit("vector",
                      lambda v: v.tensor_tensor(logit[:, pp, :], bank(5, 0, NE),
                                                rb_sb[:, :], op=ALU.add),
                      tk(tm), DVs, 1, reads=["ps5", "rb_sb"], writes=[f"logit:{pp}"])
        bank_rel[5] = tk(t1)
        t2 = sch.emit("vector",
                      lambda v: v.reduce_max(mxb[:, pp, :], logit[:, pp, :], axis=AX.X),
                      tk(sm_rel.get(("mxb", pp))), DVs, 1,
                      reads=[f"logit:{pp}"], writes=[f"mxb:{pp}"])
        t3 = sch.emit("vector",
                      lambda v: v.reduce_max(nmx[:, pp, :], logit[:, pp, :], axis=AX.X,
                                             negate=True),
                      tk(sm_rel.get(("nmx", pp))), DVs, 1,
                      reads=[f"logit:{pp}"], writes=[f"nmx:{pp}"])
        t4 = sch.emit("scalar",
                      lambda sc: sc.activation(probs[:, pp, :], logit[:, pp, :],
                                               AF.Exp, bias=nmx[:, pp, :], scale=1.0,
                                               accum_out=sume[:, pp, :]),
                      tk(t1, t3, sm_rel.get(("probs", pp)), sm_rel.get(("sume", pp))),
                      ACs, 1, reads=[f"alogit:{pp}"], writes=[f"probs:{pp}", f"sume:{pp}"])
        t5 = sch.emit("vector",
                      lambda v: v.reciprocal(gate[:, pp, :], sume[:, pp, :]),
                      tk(t4, sm_rel.get(("gate", pp))), DVs, 1,
                      reads=[f"sume:{pp}"], writes=[f"gate:{pp}"])
        t6 = sch.emit("vector",
                      lambda v: v.tensor_scalar(probs[:, pp, :], logit[:, pp, :],
                                                mxb[:, pp, 0:1], None, op0=ALU.is_equal),
                      tk(t4), DVs, 1,
                      reads=[f"logit:{pp}", f"mxb:{pp}"], writes=[f"probs:{pp}"])
        t7 = sch.emit("vector",
                      lambda v: v.tensor_scalar(pcoef[:, b, :], probs[:, pp, :],
                                                gate[:, pp, :], 0.5,
                                                op0=ALU.mult, op1=ALU.mult),
                      tk(t6), DVs, 1,
                      reads=[f"probs:{pp}", f"gate:{pp}"], writes=[f"pc:{b}"])
        sm_rel[("logit", pp)] = tk(t4, t6)
        sm_rel[("mxb", pp)] = tk(t6)
        sm_rel[("nmx", pp)] = tk(t4)
        sm_rel[("sume", pp)] = tk(t5)
        sm_rel[("gate", pp)] = tk(t7)
        sm_rel[("probs", pp)] = tk(t7)
        t8 = sch.emit("vector",
                      lambda v: v.tensor_copy(pcb[:, b, :], pcoef[:, b, :]),
                      tk(t7, sm_rel.get(("pcb", b))), DVs, 1,
                      reads=[f"pc:{b}"], writes=[f"pcb:{b}"])
        bank_rel[5] = tk(t1)

        def emit_ct():
            bb2 = tr_rot[0] % 2
            tr_rot[0] += 1
            ptc = sch.emit("tensor",
                           lambda t, bb=bb2: t.transpose(
                               PTh[0:8, 1024 * bb:1024 * bb + 128],
                               pcb[:, b, :], ident[:, :]),
                           tk(t8, tbank_rel[bb2], t_ident), PEs, 1)
            if bb2 == 0:
                dtc = sch.emit("vector",
                               lambda v, bb=bb2: v.tensor_copy(
                                   cTt[:, b, :], PTh[0:8, 1024 * bb:1024 * bb + 128]),
                               tk(ptc), DVs, 1, reads=["pt"], writes=[f"cT:{b}"])
            else:
                dtc = sch.emit("scalar",
                               lambda sc, bb=bb2: sc.copy(
                                   cTt[:, b, :], PTh[0:8, 1024 * bb:1024 * bb + 128]),
                               tk(ptc), ACs, 1, reads=["apt"], writes=[f"acT:{b}"])
            tbank_rel[bb2] = tk(dtc)
            sm_rel[("pcb", b)] = tk((PEs, PEs.count))
            return tk(dtc)
        return tk(t7), emit_ct


    ffn_rot = [0]

    def ffn_pass(b, e, parity, first, xnT_t, pc_t, ct_t, wexp_t, weave=None):
        evs = []
        ct_ticket = None
        if first:
            ct_ticket = ct_t() if callable(ct_t) else ct_t
        for fh in range(2):
            bk = ffn_rot[0] % 4
            ffn_rot[0] += 1
            wl = tk(bank_rel[bk], xnT_t, wexp_t)
            tm = None
            for k in range(KC):
                tm = sch.emit("tensor",
                              lambda t, kk=k, bb=bk, ff=fh: t.matmul(
                                  bank(bb), xnT[:, kk, b, :],
                                  wexp[:, parity, kk, 512 * ff:512 * (ff + 1)],
                                  start=(kk == 0), stop=(kk == KC - 1)),
                              wl if k == 0 else None,
                              PEs if k == KC - 1 else None, 1)
            if bk < 1:
                ev = sch.emit("vector",
                              lambda v, bb=bk, ff=fh: v.scalar_tensor_tensor(
                                  x_sb[:, b, 512 * ff:512 * (ff + 1)],
                                  bank(bb), pcoef[:, b, e:e + 1],
                                  x_sb[:, b, 512 * ff:512 * (ff + 1)],
                                  op0=ALU.mult, op1=ALU.add),
                              tk(tm, pc_t), DVs, 1,
                              reads=[f"ps{bk}", f"pc:{b}", f"x:{b}:{fh}"],
                              writes=[f"x:{b}:{fh}"])
                bank_rel[bk] = tk(ev)
            else:
                sl = bk - 1
                ta = sch.emit("scalar",
                              lambda sc, bb=bk, ss=sl: sc.activation(
                                  ffn_tmp[:, ss, :], bank(bb), AF.Copy, bias=0.0,
                                  scale=pcoef[:, b, e:e + 1]),
                              tk(tm, pc_t, sm_rel.get(("ffn_tmp", sl))), ACs, 1,
                              reads=[f"aps{bk}"], writes=[f"ffntmp:{ss}" if False else f"ffntmp:{sl}"])
                bank_rel[bk] = tk(ta)
                ev = sch.emit("vector",
                              lambda v, ss=sl, ff=fh: v.tensor_tensor(
                                  x_sb[:, b, 512 * ff:512 * (ff + 1)],
                                  x_sb[:, b, 512 * ff:512 * (ff + 1)],
                                  ffn_tmp[:, ss, :], op=ALU.add),
                              tk(ta), DVs, 1,
                              reads=[f"x:{b}:{fh}", f"ftmp:{sl}"],
                              writes=[f"x:{b}:{fh}"])
                sm_rel[("ffn_tmp", sl)] = tk(ev)
            evs.append(ev)
            if weave is not None:
                weave(fh)
        if first:
            for fh in range(2):
                bk = 4 + fh
                wl = tk(bank_rel[bk], ct_ticket, t_eb)
                tb = sch.emit("tensor",
                              lambda t, ff=fh, bb=bk: t.matmul(
                                  bank(bb), cTt[:, b, :],
                                  eb_sb[:, 512 * ff:512 * (ff + 1)],
                                  start=True, stop=True),
                              wl, PEs, 1)
                ev = sch.emit("vector",
                              lambda v, ff=fh, bb=bk: v.tensor_tensor(
                                  x_sb[:, b, 512 * ff:512 * (ff + 1)],
                                  x_sb[:, b, 512 * ff:512 * (ff + 1)],
                                  bank(bb), op=ALU.add),
                              tk(tb), DVs, 1,
                              reads=[f"ps{bk}", f"x:{b}:{fh}"], writes=[f"x:{b}:{fh}"])
                bank_rel[bk] = tk(ev)
                evs.append(ev)
        return tk(*evs)

    def store_block(s, b, ticket):
        pp = b % 2
        tc = sch.emit("scalar",
                      lambda sc: sc.copy(out_st[:, pp, :], x_sb[:, b, :]),
                      tk(ticket, outst_rel[pp]), ACs, 1,
                      reads=[f"ax:{b}"], writes=[f"outst:{pp}"])
        blk0 = (s * n_blk + b) * 128
        st = dma_s(y_d[blk0:blk0 + 128, :], out_st[:, pp, :], tk(tc))
        outst_rel[pp] = st
        return st

    n_ffn_phases = (1 if "A" in phases else 0) + (1 if "C" in phases else 0)
    total_passes = n_seq * n_ffn_phases * NE
    loads_done = [0]
    load_t = [None] * (total_passes + 1)

    def ensure_loads(upto):
        while loads_done[0] <= min(upto, total_passes - 1):
            gp = loads_done[0]
            e = gp % NE
            par = gp % 2
            load_t[gp] = dma_g(wexp[:, par, :, :], ew_d[e],
                               tk(wexp_rel[par]))
            loads_done[0] += 1

    gpass = [0]
    ln_tr_acc = []

    x1_done = [None] * n_blk
    x2_done = [None] * n_blk
    last_phase = phases[-1]

    for s in range(n_seq):
        x_t = [dma_g(x_sb[:, b, :],
                     x_d[(s * n_blk + b) * 128:(s * n_blk + b) * 128 + 128, :],
                     tk(xslot_rel[b]))
               for b in range(n_blk)]
        if s == 0:
            load_wproj()
            if "B" in phases:
                build_qkv_cache()

        xnT_t = [None] * n_blk
        xn3_t = [None] * n_blk
        pc_t = [None] * n_blk
        ct_t = [None] * n_blk

        if "A" in phases:
            ln_a = [None] * n_blk
            for p in range(NE):
                gp = gpass[0]
                ensure_loads(gp + 1)
                par = gp % 2
                if p == 0:
                    ln_a[0] = ln_stats(0, tk(x_t[0]))
                    if n_blk > 1:
                        ln_a[1] = ln_stats(1, tk(x_t[1]))
                    xnT_t[0] = ln_transposes(0, ln_a[0])
                    pc_t[0], ct_t[0] = router_block(0, xnT_t[0])
                for b in range(n_blk):
                    weave = None
                    if p == 0 and b + 1 < n_blk:
                        def weave(fh, bb1=b + 1):
                            pp1 = bb1 % 2
                            ln_tr_acc.append(transpose_burst(
                                [xn_st[:, pp1, 128 * k:128 * (k + 1)]
                                 for k in range(4 * fh, 4 * fh + 4)],
                                xnT[:, 4 * fh:4 * fh + 4, bb1, :], tk(ln_a[bb1]),
                                wkey=f"xnT:{bb1}:{fh}"))
                            if fh == 1:
                                sm_rel[("xn_st", pp1)] = tk(*ln_tr_acc,
                                                            (PEs, PEs.count))
                                xnT_t[bb1] = tk(*ln_tr_acc)
                                ln_tr_acc.clear()
                                pc_t[bb1], ct_t[bb1] = router_block(bb1, xnT_t[bb1])
                                if bb1 + 1 < n_blk:
                                    ln_a[bb1 + 1] = ln_stats(bb1 + 1, tk(x_t[bb1 + 1]))
                    ev = ffn_pass(b, p, par, p == 0, xnT_t[b], pc_t[b], ct_t[b],
                                  load_t[gp], weave=weave)
                    if p == NE - 1:
                        x1_done[b] = ev
                        if last_phase == "A":
                            xslot_rel[b] = tk(store_block(s, b, ev))
                wexp_rel[par] = tk((PEs, PEs.count))
                gpass[0] += 1
        else:
            for b in range(n_blk):
                x1_done[b] = tk(x_t[b])

        if "B" in phases:
            qkv_evt = [None] * n_blk
            ln2_t = [None] * n_blk
            ln2_s = [None] * n_blk
            qload_i = [0]

            def qkv_group(g):
                blks = [2 * g, 2 * g + 1]
                for b in blks:
                    if ln2_s[b] is None:
                        ln2_s[b] = ln_stats(b, x1_done[b])
                for b in blks:
                    ln2_t[b] = ln_transposes(b, ln2_s[b])
                all_ev = []
                for jb in range(6):
                    sl = qload_i[0] % 2
                    qload_i[0] += 1
                    fbase = 4 * jb
                    wt = dma_s(wqkv_sb[:, sl, :, :, :],
                               wqc_d[:, fbase:fbase + 4, :, :],
                               tk(wq_rel[sl], qc_box[0]))
                    for fi in range(4):
                        f = fbase + fi
                        bk = f % 2
                        wl = tk(bank_rel[bk], wt, ln2_t[blks[0]], ln2_t[blks[1]])
                        tm = None
                        for k in range(KC):
                            tm = sch.emit("tensor",
                                          lambda t, kk=k, bb=bk, ss=sl, ii=fi: t.matmul(
                                              bank(bb, 0, 256),
                                              wqkv_sb[:, ss, ii, kk, :],
                                              xnT[:, kk, 2 * g:2 * g + 2, :],
                                              start=(kk == 0), stop=(kk == KC - 1)),
                                          wl if k == 0 else None,
                                          PEs if k == KC - 1 else None, 1)
                        wq_rel[sl] = tk(wq_rel[sl], tm)
                        if f < 8:
                            eng, sem = (("vector", DVs) if bk == 0
                                        else ("scalar", ACs))
                            def _cp(d_, s_, dve=(bk == 0)):
                                def f_(e2, d=d_, ss=s_, dv_=dve):
                                    return (e2.tensor_copy(d, ss) if dv_
                                            else e2.copy(d, ss))
                                return f_
                            ev1 = sch.emit(eng, _cp(qT[:, (2 * g) % R_QT, ff, :]
                                                    if False else
                                                    qT[:, (2 * g) % R_QT, f, :],
                                                    bank(bk, 0, 128)),
                                           tk(tm, qt_rel[(2 * g) % R_QT]), sem, 1,
                                           reads=[f"ps{bk}"], writes=[f"qT:{2*g}"])
                            ev2 = sch.emit(eng, _cp(qT[:, (2 * g + 1) % R_QT, f, :],
                                                    bank(bk, 128, 128)),
                                           tk(qt_rel[(2 * g + 1) % R_QT]), sem, 1,
                                           reads=[f"ps{bk}"], writes=[f"qT:{2*g+1}"])
                            bank_rel[bk] = tk(ev2)
                            all_ev += [ev1, ev2]
                        elif f < 16:
                            eng, sem = (("vector", DVs) if bk == 0
                                        else ("scalar", ACs))
                            def _cp2(d_, s_, dve=(bk == 0)):
                                def f_(e2, d=d_, ss=s_, dv_=dve):
                                    return (e2.tensor_copy(d, ss) if dv_
                                            else e2.copy(d, ss))
                                return f_
                            ev1 = sch.emit(eng, _cp2(kT[:, (2 * g) % R_KT, f - 8, :],
                                                     bank(bk, 0, 128)),
                                           tk(tm, kt_rel[(2 * g) % R_KT]), sem, 1,
                                           reads=[f"ps{bk}"], writes=[f"kT:{2*g}"])
                            ev2 = sch.emit(eng, _cp2(kT[:, (2 * g + 1) % R_KT, f - 8, :],
                                                     bank(bk, 128, 128)),
                                           tk(kt_rel[(2 * g + 1) % R_KT]), sem, 1,
                                           reads=[f"ps{bk}"], writes=[f"kT:{2*g+1}"])
                            bank_rel[bk] = tk(ev2)
                            all_ev += [ev1, ev2]
                        else:
                            fv = f - 16
                            if bk == 0:
                                ev = sch.emit("vector",
                                              lambda v, bb=bk, ff=fv: v.tensor_copy(
                                                  vfeat[:, ff % 2, :], bank(bb, 0, 256)),
                                              tk(tm, sm_rel.get(("vfeat", fv % 2))), DVs, 1,
                                              reads=[f"ps{bk}"], writes=[f"vf:{fv%2}"])
                            else:
                                ev = sch.emit("scalar",
                                              lambda sc, bb=bk, ff=fv: sc.copy(
                                                  vfeat[:, ff % 2, :], bank(bb, 0, 256)),
                                              tk(tm, sm_rel.get(("vfeat", fv % 2))), ACs, 1,
                                              reads=[f"aps{bk}"], writes=[f"avf:{fv%2}"])
                            bank_rel[bk] = tk(ev)
                            evs_v = transpose_burst_multi(
                                [(vfeat[:, fv % 2, 128 * half:128 * (half + 1)],
                                  vtok[:, b2 % R_V, 2 * fv:2 * fv + 2, 0:64])
                                 for half, b2 in enumerate(blks)],
                                tk(ev, v_rel[blks[0] % R_V], v_rel[blks[1] % R_V],
                                   t_vones),
                                wkey=f"v:{2*g}")
                            all_ev += evs_v
                            sm_rel[("vfeat", fv % 2)] = tk((PEs, PEs.count))
                return tk(*all_ev)

            def attn_block(b, weave=None):
                if b_stop == 0:
                    qt_rel[b % R_QT] = tk((PEs, PEs.count))
                    if b >= 1:
                        kt_rel[(b - 1) % R_KT] = tk((PEs, PEs.count))
                        v_rel[(b - 1) % R_V] = tk((PEs, PEs.count))
                    if b == n_blk - 1:
                        kt_rel[b % R_KT] = tk((PEs, PEs.count))
                        v_rel[b % R_V] = tk((PEs, PEs.count))
                    return tk(qkv_evt[b], x1_done[b])
                chunks = [c for c in range(3) if 0 <= b + c - 1 < n_blk]
                nch = len(chunks)
                pp = b % 2
                exp_t = [None] * H
                eslot = [0] * H
                wl_qkv = tk(qkv_evt[b],
                            qkv_evt[b + 1] if b + 1 < n_blk else None,
                            qkv_evt[b - 1] if b - 1 >= 0 else None)
                x2e = []
                for half in range(2):
                    heads = range(8 * half, 8 * half + 8)
                    for pi in range(4):
                        gpair = 4 * half + pi
                        bkA = 4 if gpair % 2 == 0 else 2
                        pslot = (b * 8 + gpair) % (R_EKM // 2)
                        tmp_ = [None, None]
                        for sub in range(2):
                            h = 8 * half + 2 * pi + sub
                            eslot[h] = 2 * pslot + sub
                            fc = h // 2
                            bk = bkA + sub
                            part = slice(64 * sub, 64 * sub + 64)
                            wl = tk(bank_rel[bk], wl_qkv)
                            tm = None
                            for i, c in enumerate(chunks):
                                kb = b + c - 1
                                tm = sch.emit("tensor",
                                              lambda t, ii=i, hh=h, bb=bk, ps=part,
                                              kk=kb, ff=fc: t.matmul(
                                                  bank(bb, 128 * ii, 128),
                                                  kT[ps, kk % R_KT, ff, :],
                                                  qT[ps, b % R_QT, ff, :],
                                                  start=True, stop=True),
                                              wl if i == 0 else None,
                                              PEs if i == nch - 1 else None, 1)
                            tmp_[sub] = tm
                        te0 = sch.emit("scalar",
                                       lambda sc, bb=bkA, ss=2 * pslot, nn=nch: sc.activation(
                                           ekm[:, ss, 0:128 * nn], bank(bb, 0, 128 * nn),
                                           AF.Exp, bias=zc[:, :], scale=0.125),
                                       tk(tmp_[0], ekm_rel[2 * pslot], t_cst3), ACs, 1,
                                       reads=[f"aps{bkA}"], writes=[f"ekm:{2*pslot}"])
                        te = sch.emit("scalar",
                                      lambda sc, bb=bkA + 1, ss=2 * pslot + 1, nn=nch: sc.activation(
                                          ekm[:, ss, 0:128 * nn], bank(bb, 0, 128 * nn),
                                          AF.Exp, bias=zc[:, :], scale=0.125),
                                      tk(tmp_[1], ekm_rel[2 * pslot + 1], t_cst3), ACs, 1,
                                      reads=[f"aps{bkA+1}"], writes=[f"ekm:{2*pslot+1}"])
                        bank_rel[bkA] = tk(te0)
                        bank_rel[bkA + 1] = tk(te)
                        te_by_sub = [te0, te]
                        for sub in range(2):
                            h = 8 * half + 2 * pi + sub
                            ss2 = 2 * pslot + sub
                            te = te_by_sub[sub]
                            mt = [None] * nch
                            for i, c in enumerate(chunks):
                                if c == 1:
                                    continue
                                m = mlo if c == 0 else mhi
                                mt[i] = sch.emit(
                                    "vector",
                                    lambda v, s2=ss2, ii=i, mm=m: v.tensor_tensor(
                                        ekm[:, s2, 128 * ii:128 * (ii + 1)],
                                        ekm[:, s2, 128 * ii:128 * (ii + 1)],
                                        mm[:, :], op=ALU.mult),
                                    tk(te, t_mlo, t_mhi), DVs, 1,
                                    reads=[f"ekm:{ss2}"], writes=[f"ekm:{ss2}"])
                            exp_t[h] = (tk(te), [tk(te) if x is None else tk(x)
                                                 for x in mt])
                    if b_stop == 1:
                        continue
                    tmh = {}
                    for h in heads:
                        bk = 2 + ((h % 8) // 4)
                        off = 128 * (h % 4)
                        te_t, mt = exp_t[h]
                        order = sorted(range(nch), key=lambda i: chunks[i] != 1)
                        base = tk(bank_rel[bk], te_t,
                                  *[v_rel[(b + c - 1) % R_V] for c in chunks])
                        tm = None
                        for j, i in enumerate(order):
                            kb = b + chunks[i] - 1
                            wl = tk(base, mt[i]) if j == 0 else tk(mt[i])
                            tm = sch.emit("tensor",
                                          lambda t, hh=h, ii=i, bb=bk, oo=off,
                                          kk=kb, s0=(j == 0), s1=(j == nch - 1): t.matmul(
                                              bank(bb, oo, 65),
                                              ekm[:, eslot[hh], 128 * ii:128 * (ii + 1)],
                                              vtok[:, kk % R_V, hh, :],
                                              start=s0, stop=s1),
                                          wl,
                                          PEs if j == nch - 1 else None, 1)
                        tmh[h] = tm
                    for h in heads:
                        ekm_rel[eslot[h]] = tk(tmh[h])
                    if weave is not None:
                        weave(half)
                    rec_t = {}
                    for jb, bk in enumerate((2, 3)):
                        h0 = 8 * half + 4 * jb
                        wl = tk(*[tmh[h] for h in range(h0, h0 + 4)],
                                sm_rel.get(("invl", pp, jb)))
                        rec = sch.emit("vector",
                                       lambda v, bb=bk, hh=h0: v.reciprocal(
                                           invl[:, pp, hh:hh + 4],
                                           bass.AP(PSh, 512 * bb + 64, [[3072, 128], [128, 4]])),
                                       wl, DVs, 1,
                                       reads=[f"ps{bk}"], writes=[f"invl:{pp}:{jb}"])
                        rec_t[bk] = rec
                    evh = []
                    for h in heads:
                        bk = 2 + ((h % 8) // 4)
                        off = 128 * (h % 4)
                        rt = rec_t[bk]
                        if bk == 2:
                            ev = sch.emit("vector",
                                          lambda v, hh=h, bb=bk, oo=off: v.tensor_scalar(
                                              anorm[:, pp, 64 * hh:64 * (hh + 1)],
                                              bank(bb, oo, 64),
                                              invl[:, pp, hh:hh + 1], None, op0=ALU.mult),
                                          tk(rt, sm_rel.get(("anorm", pp))), DVs, 1,
                                          reads=[f"ps{bk}", f"invl:{pp}:0"],
                                          writes=[f"anorm:{pp}:{h}"])
                        else:
                            ev = sch.emit("scalar",
                                          lambda sc, hh=h, bb=bk, oo=off: sc.activation(
                                              anorm[:, pp, 64 * hh:64 * (hh + 1)],
                                              bank(bb, oo, 64), AF.Copy, bias=0.0,
                                              scale=invl[:, pp, hh:hh + 1]),
                                          tk(rt, sm_rel.get(("anorm", pp))), ACs, 1,
                                          reads=[f"aps{bk}", f"ainvl:{pp}"],
                                          writes=[f"anorm:{pp}:{h}"])
                        evh.append(ev)
                        bank_rel[bk] = tk(bank_rel[bk], ev)
                    x2e += evh
                if b_stop == 1:
                    ee = tk(*[exp_t[h] for h in range(H)], x1_done[b], qkv_evt[b])
                    qt_rel[b % R_QT] = tk((PEs, PEs.count))
                    if b >= 1:
                        kt_rel[(b - 1) % R_KT] = tk((PEs, PEs.count))
                        v_rel[(b - 1) % R_V] = tk((PEs, PEs.count))
                    if b == n_blk - 1:
                        kt_rel[b % R_KT] = tk((PEs, PEs.count))
                        v_rel[b % R_V] = tk((PEs, PEs.count))
                    for h in range(H):
                        ekm_rel[eslot[h]] = ee
                    return ee
                sm_rel[("invl", pp, 0)] = tk(*x2e)
                sm_rel[("invl", pp, 1)] = tk(*x2e)
                if b_stop == 2:
                    ee = tk(*x2e, x1_done[b])
                    qt_rel[b % R_QT] = tk((PEs, PEs.count))
                    if b >= 1:
                        kt_rel[(b - 1) % R_KT] = tk((PEs, PEs.count))
                        v_rel[(b - 1) % R_V] = tk((PEs, PEs.count))
                    if b == n_blk - 1:
                        kt_rel[b % R_KT] = tk((PEs, PEs.count))
                        v_rel[b % R_V] = tk((PEs, PEs.count))
                    return ee
                aT_l = []
                for h4 in range(2):
                    aT_l.append(transpose_burst(
                        [anorm[:, pp, 128 * k:128 * (k + 1)]
                         for k in range(4 * h4, 4 * h4 + 4)],
                        aT[:, pp, 4 * h4:4 * h4 + 4, :], tk(*x2e),
                        wkey=f"aT:{pp}:{h4}"))
                aT_t = tk(*aT_l)
                sm_rel[("anorm", pp)] = tk(aT_t, (PEs, PEs.count))
                pevs = []
                for fh in range(2):
                    bk = fh
                    wl = tk(bank_rel[bk], aT_t, t_wproj_box[0], sm_rel.get(("aTr", pp)))
                    tm = None
                    for k in range(KC):
                        tm = sch.emit("tensor",
                                      lambda t, kk=k, bb=bk, ff=fh: t.matmul(
                                          bank(bb), aT[:, pp, kk, :],
                                          wproj_sb[:, kk, 512 * ff:512 * (ff + 1)],
                                          start=(kk == 0), stop=(kk == KC - 1)),
                                      wl if k == 0 else None,
                                      PEs if k == KC - 1 else None, 1)
                    ev = sch.emit("vector",
                                  lambda v, bb=bk, ff=fh: v.tensor_tensor(
                                      x_sb[:, b, 512 * ff:512 * (ff + 1)],
                                      x_sb[:, b, 512 * ff:512 * (ff + 1)],
                                      bank(bb), op=ALU.add),
                                  tk(tm), DVs, 1,
                                  reads=[f"ps{bk}", f"x:{b}:{fh}"], writes=[f"x:{b}:{fh}"])
                    bank_rel[bk] = tk(ev)
                    pevs.append(ev)
                sm_rel[("aTr", pp)] = tk((PEs, PEs.count))
                tbp = sch.emit("vector",
                               lambda v: v.tensor_tensor(x_sb[:, b, :], x_sb[:, b, :],
                                                         bproj_sb[:, :], op=ALU.add),
                               tk(t_bp, *pevs), DVs, 1,
                               reads=["bproj_sb", f"x:{b}:0", f"x:{b}:1"], writes=[f"x:{b}:0", f"x:{b}:1"])
                qt_rel[b % R_QT] = tk(tbp, (PEs, PEs.count))
                if b >= 1:
                    kt_rel[(b - 1) % R_KT] = tk((PEs, PEs.count))
                    v_rel[(b - 1) % R_V] = tk((PEs, PEs.count))
                if b == n_blk - 1:
                    kt_rel[b % R_KT] = tk((PEs, PEs.count))
                    v_rel[b % R_V] = tk((PEs, PEs.count))
                return tk(tbp)

            attn_next = 0
            for g in range(n_blk // 2):
                gt = qkv_group(g)
                qkv_evt[2 * g] = gt
                qkv_evt[2 * g + 1] = gt
                while attn_next < n_blk and (
                        attn_next + 1 >= n_blk or qkv_evt[attn_next + 1] is not None):
                    bb_ = attn_next
                    weave = None
                    if last_phase != "B" and "C" in phases and bb_ >= 1:
                        bprev = bb_ - 1
                        st3 = ln_stats(bprev, x2_done[bprev])
                        tr3 = []

                        def weave(half, bp=bprev, st=st3, trl=tr3):
                            pp3 = bp % 2
                            trl.append(transpose_burst(
                                [xn_st[:, pp3, 128 * k:128 * (k + 1)]
                                 for k in range(4 * half, 4 * half + 4)],
                                xnT[:, 4 * half:4 * half + 4, bp, :], tk(st),
                                wkey=f"xnT:{bp}:{half}"))
                            if half == 1:
                                sm_rel[("xn_st", pp3)] = tk(*trl, (PEs, PEs.count))
                                xn3_t[bp] = tk(*trl)
                                pc_t[bp], ct_t[bp] = router_block(bp, xn3_t[bp])
                    x2_done[bb_] = attn_block(bb_, weave=weave)
                    if last_phase == "B":
                        xslot_rel[bb_] = tk(store_block(s, bb_, x2_done[bb_]))
                    attn_next += 1
            if last_phase != "B" and "C" in phases and n_blk >= 1:
                bl = n_blk - 1
                xn3_t[bl] = ln_block(bl, x2_done[bl], "C")
                pc_t[bl], ct_t[bl] = router_block(bl, xn3_t[bl])
        else:
            for b in range(n_blk):
                x2_done[b] = x1_done[b]

        if "C" in phases:
            p_start = DBG_P_START if ("B" in phases and last_phase != "B" and not DBG_NO_C0) else 0
            for p in range(p_start, NE):
                gp = gpass[0]
                ensure_loads(gp + 1)
                par = gp % 2
                for b in range(n_blk):
                    if p == p_start and xn3_t[b] is None:
                        xn3_t[b] = ln_block(b, x2_done[b], "C")
                        pc_t[b], ct_t[b] = router_block(b, xn3_t[b])
                    ev = ffn_pass(b, p, par, (p == 0) or (p == 1 and not DBG_C0_BIAS
                                               and p_start == 1), xn3_t[b], pc_t[b],
                                  ct_t[b], load_t[gp])
                    if p == NE - 1:
                        xslot_rel[b] = tk(store_block(s, b, ev))
                wexp_rel[par] = tk((PEs, PEs.count))
                gpass[0] += 1

        if s == 0:
            for nm, shp in taps:
                assert shp == [n_blk * 128, DM]
                for b in range(n_blk):
                    dma_g(tap_d[nm][b * 128:(b + 1) * 128, :], x_sb[:, b, :],
                          tk(xslot_rel[b] if last_phase == "C" else x2_done[b]))

    if DBG_DUMP_PC:
        dbg_pc = nc.dram_tensor("dbg_pc", [n_blk * 128, NE], F32, kind="ExternalOutput").ap()
        dbg_ct = nc.dram_tensor("dbg_ct", [NE * n_blk, 128], F32, kind="ExternalOutput").ap()
        for b in range(n_blk):
            dma_g(dbg_pc[b * 128:(b + 1) * 128, :], pcoef[:, b, :],
                  tk((PEs, PEs.count), (DVs, DVs.count)))
        dma_g(dbg_ct.rearrange("(e b) f -> e b f", e=NE), cTt[:, :, :],
              tk((PEs, PEs.count), (DVs, DVs.count), (ACs, ACs.count)))
        dbg_x = nc.dram_tensor("dbg_x", [n_blk * 128, DM], F32, kind="ExternalOutput").ap()
        for b in range(n_blk):
            dma_g(dbg_x[b * 128:(b + 1) * 128, :], x_sb[:, b, :],
                  tk((PEs, PEs.count), (DVs, DVs.count), (ACs, ACs.count)))
        dbg_xnt = nc.dram_tensor("dbg_xnt", [128, KC, n_blk, 128], F32, kind="ExternalOutput").ap()
        dma_g(dbg_xnt, xnT[:, :, :, :], tk((PEs, PEs.count), (DVs, DVs.count), (ACs, ACs.count)))

    sch.emit("sync", lambda g: g.nop(),
             tk(*[(l, l.count) for l in DS_l], *[(l, l.count) for l in DG_l]), None)

    for sem in sch.sems.values():
        cm = nc.semaphore(sem.name)
        sem.handle = cm.__enter__()
        ctx.append(cm)
    with nc.Block() as block:
        block.tensor(lambda t: sch.replay_engine("tensor", t))
        block.vector(lambda v: sch.replay_engine("vector", v))
        block.scalar(lambda sc: sch.replay_engine("scalar", sc))
        block.gpsimd(lambda g: sch.replay_engine("gpsimd", g))
        block.sync(lambda sy: sch.replay_engine("sync", sy))

    for t in reversed(ctx):
        t.__exit__(None, None, None)
    assert sbuf_bytes[0] <= 207 * 1024, f"SBUF over budget: {sbuf_bytes[0]}"
    build_stats = dict(steps=len(sch.steps), drains=sch.n_drains,
                       sbuf=sbuf_bytes[0])
    return nc, build_stats


def _pm_kf(w):
    f = w.shape[1]
    return np.ascontiguousarray(w.reshape(KC, 128, f).transpose(1, 0, 2))


def _pm_qkv(w):
    return np.ascontiguousarray(
        w.reshape(KC, 128, 24, 128).transpose(1, 2, 0, 3))


def _pm_ew(w):
    return np.ascontiguousarray(
        w.reshape(NE, KC, 128, DM).transpose(0, 2, 1, 3).reshape(NE, 128, KC * DM))


def _consts():
    import ml_dtypes
    bfd = ml_dtypes.bfloat16
    ident = np.eye(128, dtype=np.float32).astype(bfd)
    r = np.arange(128)
    mlo = (r[:, None] >= r[None, :]).astype(np.float32).astype(bfd)
    mhi = (r[:, None] <= r[None, :]).astype(np.float32).astype(bfd)
    return ident, mlo, mhi


def make_in_map(x_shard, w):
    ident, mlo, mhi = _consts()
    return {
        "x": np.ascontiguousarray(x_shard.reshape(-1, DM)),
        "wqkv": w["wqkv"], "wproj": w["wproj"], "bproj": w["bproj"],
        "rw": w["rw"], "rb": w["rb"], "ew": w["ew"], "eb": w["eb"],
        "identd": ident, "mlod": mlo, "mhid": mhi,
    }


def kernel(L0_attn_x, L0_attn_w_qkv, L0_attn_w_proj, L0_attn_b_proj,
           L0_ffn_router_w, L0_ffn_router_b, L0_ffn_expert_w, L0_ffn_expert_b):
    x = np.asarray(L0_attn_x, dtype=np.float32)
    w = {
        "wqkv": _pm_qkv(np.asarray(L0_attn_w_qkv, dtype=np.float32)),
        "wproj": _pm_kf(np.asarray(L0_attn_w_proj, dtype=np.float32)),
        "bproj": np.ascontiguousarray(L0_attn_b_proj, dtype=np.float32).reshape(1, DM),
        "rw": _pm_kf(np.asarray(L0_ffn_router_w, dtype=np.float32)),
        "rb": np.ascontiguousarray(L0_ffn_router_b, dtype=np.float32).reshape(1, NE),
        "ew": _pm_ew(np.asarray(L0_ffn_expert_w, dtype=np.float32)),
        "eb": np.ascontiguousarray(L0_ffn_expert_b, dtype=np.float32),
    }
    nc, _ = build_nc()
    in_maps = [make_in_map(x[SEQ_PER_CORE * c:SEQ_PER_CORE * (c + 1)], w)
               for c in range(NCORE)]
    res = run_bass_kernel_spmd(nc, in_maps, core_ids=list(range(NCORE)))
    out = np.concatenate(
        [np.asarray(r["y"]).reshape(SEQ_PER_CORE, S, DM) for r in res.results],
        axis=0)
    return out.astype(np.float32)


# version 22
# speedup vs baseline: 1.0518x; 1.0518x over previous
import numpy as np
import concourse.bass as bass
import concourse.mybir as mybir
from concourse.bass_utils import run_bass_kernel_spmd

F32 = mybir.dt.float32
BF16 = mybir.dt.bfloat16
AF = mybir.ActivationFunctionType
ALU = mybir.AluOpType
AX = mybir.AxisListType

B_TOT, S, DM = 32, 2048, 1024
DBG_C0_INLOOP = False
DBG_C0_BIAS = True
DBG_NO_C0 = True
DBG_P_START = 1
DBG_DUMP_PC = False
H, DH = 16, 64
NCORE = 8
SEQ_PER_CORE = B_TOT // NCORE
NBLK = S // 128
KC = DM // 128
NE = 8
EPS = 1e-5

R_KT = 6
R_V = 6
R_QT = 4
R_EKM = 10


class Sem:
    __slots__ = ("name", "count", "handle")

    def __init__(self, name):
        self.name = name
        self.count = 0
        self.handle = None


def tk(*parts):
    best = {}
    for p in parts:
        if p is None:
            continue
        if isinstance(p, tuple):
            p = [p]
        for s, v in p:
            if best.get(s, 0) < v:
                best[s] = v
    return list(best.items())


class Sched:
    def __init__(self):
        self.steps = []
        self.sems = {}
        self.n_drains = 0

    def sem(self, name):
        if name not in self.sems:
            self.sems[name] = Sem(name)
        return self.sems[name]

    def emit(self, eng, fn, waits=None, inc=None, n=1, reads=(), writes=()):
        w = tk(waits)
        t = None
        if inc is not None:
            inc.count += n
            t = (inc, inc.count)
        self.steps.append((eng, fn, w, (inc, n) if inc else None,
                           tuple(reads), tuple(writes)))
        return t

    def replay_engine(self, eng_name, eng):
        last_wait = {}
        wpos = {}
        rpos = {}
        idx = 0
        cleared_below = -10**9
        WINDOW = 10
        for en, fn, waits, inc, reads, writes in self.steps:
            if en != eng_name:
                continue
            if eng_name in ("vector", "scalar"):
                idx += 1
                hz = max(
                    [wpos.get(r, -10**9) for r in reads]
                    + [max(wpos.get(w, -10**9), rpos.get(w, -10**9)) for w in writes]
                    + [-10**9])
                if hz >= cleared_below and idx - hz <= WINDOW:
                    eng.drain()
                    self.n_drains += 1
                    cleared_below = idx
                for w in writes:
                    wpos[w] = idx
                for r in reads:
                    rpos[r] = idx
            for sem, val in waits:
                if last_wait.get(sem.name, 0) >= val:
                    continue
                eng.wait_ge(sem.handle, val)
                last_wait[sem.name] = val
            inst = fn(eng)
            if inc is not None:
                sem, nn = inc
                inst.then_inc(sem.handle, nn)


def build_nc(n_seq=SEQ_PER_CORE, n_blk=NBLK, phases=("A", "B", "C"), taps=(), b_stop=3):
    nc = bass.Bass("TRN2", target_bir_lowering=False)
    NT = n_seq * n_blk * 128

    x_d = nc.dram_tensor("x", [NT, DM], F32, kind="ExternalInput").ap()
    y_d = nc.dram_tensor("y", [NT, DM], F32, kind="ExternalOutput").ap()
    wqkv_d = nc.dram_tensor("wqkv", [128, 24, KC, 128], F32, kind="ExternalInput").ap()
    wproj_d = nc.dram_tensor("wproj", [128, KC, DM], F32, kind="ExternalInput").ap()
    bproj_d = nc.dram_tensor("bproj", [1, DM], F32, kind="ExternalInput").ap()
    rw_d = nc.dram_tensor("rw", [128, KC, NE], F32, kind="ExternalInput").ap()
    rb_d = nc.dram_tensor("rb", [1, NE], F32, kind="ExternalInput").ap()
    ew_d = nc.dram_tensor("ew", [NE, 128, KC * DM], F32, kind="ExternalInput").ap()
    eb_d = nc.dram_tensor("eb", [NE, DM], F32, kind="ExternalInput").ap()
    ident_d = nc.dram_tensor("identd", [128, 128], BF16, kind="ExternalInput").ap()
    wqc_d = nc.dram_tensor("wqc", [128, 24, KC, 128], BF16,
                           kind="ExternalOutput").ap()
    mlo_d = nc.dram_tensor("mlod", [128, 128], BF16, kind="ExternalInput").ap()
    mhi_d = nc.dram_tensor("mhid", [128, 128], BF16, kind="ExternalInput").ap()
    tap_d = {nm: nc.dram_tensor("tap_" + nm, shp, F32, kind="ExternalOutput").ap()
             for nm, shp in taps}

    sch = Sched()
    PEs = sch.sem("pe")
    DVs = sch.sem("dv")
    ACs = sch.sem("ac")
    NLANE = 4
    DG_l = [sch.sem(f"dg{i}") for i in range(NLANE)]
    DS_l = [sch.sem(f"ds{i}") for i in range(NLANE)]
    dma_rr = {"g": 0, "s": 0}

    ctx = []
    sbuf_bytes = [0]

    def sb(name, shape, dtype):
        t = nc.sbuf_tensor(name, shape, dtype)
        h = t.__enter__()
        ctx.append(t)
        per_part = int(np.prod(shape[1:])) * (2 if dtype == BF16 else 4)
        sbuf_bytes[0] += per_part
        return h

    x_sb = sb("x_sb", [128, n_blk, DM], BF16)
    xnT = sb("xnT", [128, KC, n_blk, 128], BF16)
    xn_st = sb("xn_st", [128, 2, DM], BF16)
    wexp = sb("wexp", [128, 2, KC, DM], BF16)
    wqkv_sb = sb("wqkv_sb", [128, 2, 4, KC, 128], BF16)
    wproj_sb = sb("wproj_sb", [128, KC, DM], BF16)
    rw_sb = sb("rw_sb", [128, KC, NE], BF16)
    eb_sb = sb("eb_sb", [NE, DM], BF16)
    rb_sb = sb("rb_sb", [128, NE], F32)
    rb_row = sb("rb_row", [1, NE], F32)
    bproj_sb = sb("bproj_sb", [128, DM], BF16)
    onesr = sb("onesr", [1, 128], F32)
    epsc = sb("epsc", [128, 1], F32)
    zc = sb("zc", [128, 1], F32)
    ident = sb("ident", [128, 128], BF16)
    mlo = sb("mlo", [128, 128], BF16)
    mhi = sb("mhi", [128, 128], BF16)
    qT = sb("qT", [128, R_QT, KC, 128], BF16)
    kT = sb("kT", [128, R_KT, KC, 128], BF16)
    vtok = sb("vtok", [128, R_V, H, 65], BF16)
    ekm = sb("ekm", [128, R_EKM, 384], BF16)
    vfeat = sb("vfeat", [128, 2, 256], BF16)
    anorm = sb("anorm", [128, 2, DM], BF16)
    aT = sb("aT", [128, 2, KC, 128], BF16)
    out_st = sb("out_st", [128, 2, DM], F32)
    ffn_tmp = sb("ffn_tmp", [128, 3, 512], BF16)
    mv6 = sb("mv6", [128, 2, 12], F32)
    mva = sb("mva", [128, 2, 2], F32)
    lnv = sb("lnv", [128, 2, 1], F32)
    inv = sb("inv", [128, 2, 1], F32)
    logit = sb("logit", [128, 2, NE], F32)
    mxb = sb("mxb", [128, 2, 1], F32)
    nmx = sb("nmx", [128, 2, 1], F32)
    sume = sb("sume", [128, 2, 1], F32)
    gate = sb("gate", [128, 2, 1], F32)
    probs = sb("probs", [128, 2, NE], F32)
    pcoef = sb("pcoef", [128, n_blk, NE], F32)
    cTt = sb("cTt", [NE, n_blk, 128], BF16)
    invl = sb("invl", [128, 2, H], F32)
    pcb = sb("pcb", [128, n_blk, NE], BF16)


    PS = nc.psum_tensor("PS", [128, 3072], F32)
    PSh = PS.__enter__(); ctx.append(PS)
    PT = nc.psum_tensor("PT", [128, 2048], BF16)
    PTh = PT.__enter__(); ctx.append(PT)

    def bank(i, off=0, width=512):
        assert 0 <= i < 6 and off + width <= 512
        return PSh[:, 512 * i + off: 512 * i + off + width]

    def tbank(i, off=0, width=128):
        assert 0 <= i < 2 and off + width <= 1024
        return PTh[:, 1024 * i + off: 1024 * i + off + width]

    bank_rel = [None] * 6
    tbank_rel = [None] * 2
    xslot_rel = [None] * n_blk
    kt_rel = [None] * R_KT
    v_rel = [None] * R_V
    qt_rel = [None] * R_QT
    ekm_rel = [None] * R_EKM
    wexp_rel = [None, None]
    wq_rel = [None, None]
    outst_rel = [None, None]
    sm_rel = {}

    def _dma(ring, eng, lanes, dst, src, waits):
        i = dma_rr[ring]
        dma_rr[ring] += 1
        lane = lanes[i % NLANE]
        return sch.emit(eng, lambda g, d=dst, s=src: g.dma_start(d, s),
                        tk(waits, (lane, lane.count)), lane, 16)

    def dma_g(dst, src, waits=None, tag=""):
        return _dma("g", "gpsimd", DG_l, dst, src, waits)

    def dma_s(dst, src, waits=None, tag=""):
        return _dma("s", "sync", DS_l, dst, src, waits)

    t_ident = dma_s(ident[:, :], ident_d)
    t_mlo = dma_s(mlo[:, :], mlo_d)
    t_mhi = dma_s(mhi[:, :], mhi_d)
    t_rw = dma_g(rw_sb[:, :, :], rw_d)
    t_eb = dma_g(eb_sb[:, :], eb_d)
    t_rbr = dma_g(rb_row[:, :], rb_d)
    t_bpr = dma_g(out_st[0:1, 0, :], bproj_d)
    t_wproj_box = [None]

    def load_wproj():
        t_wproj_box[0] = dma_g(wproj_sb[:, :, :], wproj_d)
        return t_wproj_box[0]
    t_cst = sch.emit("vector", lambda v: v.memset(onesr[:, :], 1.0), None, DVs, 1,
                     writes=["onesr"])
    t_cst2 = sch.emit("vector", lambda v: v.memset(epsc[:, :], EPS), None, DVs, 1,
                      writes=["epsc"])
    t_cst3 = sch.emit("vector", lambda v: v.memset(zc[:, :], 0.0), None, DVs, 1,
                      writes=["zc"])
    t_vones = sch.emit("vector", lambda v: v.memset(vtok[:, :, :, 64:65], 1.0),
                       None, DVs, 1, writes=["vones"])
    t_b1 = sch.emit("tensor",
                    lambda t: t.matmul(bank(0, 0, NE), onesr[:, :], rb_row[:, :],
                                       start=True, stop=True),
                    tk(t_cst, t_rbr), PEs, 1)
    t_rb = sch.emit("vector", lambda v: v.tensor_copy(rb_sb[:, :], bank(0, 0, NE)),
                    tk(t_b1), DVs, 1, reads=["ps0"], writes=["rb_sb"])
    bank_rel[0] = tk(t_rb)
    t_b2 = sch.emit("tensor",
                    lambda t: t.matmul(bank(1), onesr[:, :], out_st[0:1, 0, 0:512],
                                       start=True, stop=True),
                    tk(t_cst, t_bpr), PEs, 1)
    t_b3 = sch.emit("tensor",
                    lambda t: t.matmul(bank(2), onesr[:, :], out_st[0:1, 0, 512:1024],
                                       start=True, stop=True),
                    tk(t_b2), PEs, 1)
    t_bp1 = sch.emit("vector", lambda v: v.tensor_copy(bproj_sb[:, 0:512], bank(1)),
                     tk(t_b3), DVs, 1, reads=["ps1"], writes=["bproj_sb"])
    t_bp2 = sch.emit("vector", lambda v: v.tensor_copy(bproj_sb[:, 512:1024], bank(2)),
                     tk(t_b3), DVs, 1, reads=["ps2"], writes=["bproj_sb"])
    bank_rel[1] = tk(t_bp1)
    bank_rel[2] = tk(t_bp2)
    t_bp = tk(t_bp1, t_bp2)
    outst_rel[0] = tk(t_b3)


    qc_box = [None]

    def build_qkv_cache():
        qc_store = [None] * 6
        for jb in range(6):
            sl = jb % 2
            tl = dma_g(wqkv_sb[:, sl, :, :, :],
                       wqkv_d[:, 4 * jb:4 * jb + 4, :, :],
                       tk(qc_store[jb - 2] if jb >= 2 else None))
            qc_store[jb] = dma_s(wqc_d[:, 4 * jb:4 * jb + 4, :, :],
                                 wqkv_sb[:, sl, :, :, :], tk(tl))
        qc_box[0] = tk(*qc_store)

    tr_rot = [0]
    trb_rot = [0]

    def transpose_burst(srcs, dst_ap, waits, wkey):
        n = len(srcs)
        bank_i = trb_rot[0] % 2
        trb_rot[0] += 1
        wl = tk(tbank_rel[bank_i], waits, t_ident)
        pt = None
        for j, s_ap in enumerate(srcs):
            pt = sch.emit("tensor",
                          lambda t, ss=s_ap, bb=bank_i, jj=j: t.transpose(
                              PTh[:, 1024 * bb + 128 * jj: 1024 * bb + 128 * (jj + 1)],
                              ss, ident[:, :]),
                          wl if j == 0 else None,
                          PEs if j == n - 1 else None, 1)
        src_ap = PTh[:, 1024 * bank_i: 1024 * bank_i + 128 * n]
        if bank_i == 0:
            ev = sch.emit("vector",
                          lambda v, d=dst_ap, s2=src_ap: v.tensor_copy(d, s2),
                          tk(pt), DVs, 1, reads=["pt"], writes=[wkey])
        else:
            ev = sch.emit("scalar",
                          lambda sc, d=dst_ap, s2=src_ap: sc.copy(d, s2),
                          tk(pt), ACs, 1, reads=["apt"], writes=["a" + wkey])
        tbank_rel[bank_i] = tk(ev)
        return ev

    def transpose_burst_multi(pairs, waits, wkey):
        n = len(pairs)
        bank_i = trb_rot[0] % 2
        trb_rot[0] += 1
        wl = tk(tbank_rel[bank_i], waits, t_ident)
        pt = None
        for j, (s_ap, _) in enumerate(pairs):
            pt = sch.emit("tensor",
                          lambda t, ss=s_ap, bb=bank_i, jj=j: t.transpose(
                              PTh[:, 1024 * bb + 128 * jj: 1024 * bb + 128 * (jj + 1)],
                              ss, ident[:, :]),
                          wl if j == 0 else None,
                          PEs if j == n - 1 else None, 1)
        evs = []
        for j, (_, d_ap) in enumerate(pairs):
            s2 = PTh[:, 1024 * bank_i + 128 * j: 1024 * bank_i + 128 * (j + 1)]
            if bank_i == 0:
                evs.append(sch.emit("vector",
                                    lambda v, d=d_ap, ss=s2: v.tensor_copy(d, ss),
                                    tk(pt), DVs, 1, reads=["pt"],
                                    writes=[f"{wkey}:{j}"]))
            else:
                evs.append(sch.emit("scalar",
                                    lambda sc, d=d_ap, ss=s2: sc.copy(d, ss),
                                    tk(pt), ACs, 1, reads=["apt"],
                                    writes=[f"a{wkey}:{j}"]))
        tbank_rel[bank_i] = tk(*evs)
        return evs

    def transpose128(src_ap, dst_ap, waits, rkey, wkey):
        b = tr_rot[0] % 2
        tr_rot[0] += 1
        pt = sch.emit("tensor",
                      lambda t, s=src_ap, bb=b: t.transpose(tbank(bb), s, ident[:, :]),
                      tk(tbank_rel[b], waits, t_ident), PEs, 1)
        dt_ = sch.emit("vector",
                       lambda v, d=dst_ap, bb=b: v.tensor_copy(d, tbank(bb)),
                       tk(pt), DVs, 1, reads=[rkey], writes=[wkey])
        tbank_rel[b] = tk(dt_)
        return dt_

    def ln_stats(b, xin_ticket):
        pp = b % 2
        xk = f"x:{b}"
        t1 = sch.emit("vector",
                      lambda v: v.bn_stats(mv6[:, pp, 0:6], x_sb[:, b, 0:512]),
                      tk(xin_ticket, sm_rel.get(("mv6", pp))), DVs, 1,
                      reads=[xk + ":0", xk + ":1"], writes=[f"mv6:{pp}"])
        t2 = sch.emit("vector",
                      lambda v: v.bn_stats(mv6[:, pp, 6:12], x_sb[:, b, 512:1024]),
                      None, DVs, 1, reads=[xk + ":0", xk + ":1"], writes=[f"mv6:{pp}"])
        t3 = sch.emit("vector",
                      lambda v: v.bn_aggr(mva[:, pp, :], mv6[:, pp, :]),
                      tk(sm_rel.get(("mva", pp))), DVs, 1,
                      reads=[f"mv6:{pp}"], writes=[f"mva:{pp}"])
        t4 = sch.emit("scalar",
                      lambda sc: sc.activation(lnv[:, pp, :], mva[:, pp, 1:2],
                                               AF.Ln, bias=epsc[:, :], scale=1.0),
                      tk(t3, t_cst2, sm_rel.get(("lnv", pp))), ACs, 1,
                      reads=[f"amva:{pp}"], writes=[f"lnv:{pp}"])
        t5 = sch.emit("scalar",
                      lambda sc: sc.activation(inv[:, pp, :], lnv[:, pp, :],
                                               AF.Exp, bias=zc[:, :], scale=-0.5),
                      tk(t_cst3, sm_rel.get(("inv", pp))), ACs, 1,
                      reads=[f"lnv:{pp}"], writes=[f"inv:{pp}"])
        t6 = sch.emit("vector",
                      lambda v: v.tensor_scalar(xn_st[:, pp, :], x_sb[:, b, :],
                                                mva[:, pp, 0:1], inv[:, pp, :],
                                                op0=ALU.subtract, op1=ALU.mult),
                      tk(t5, sm_rel.get(("xn_st", pp))), DVs, 1,
                      reads=[xk + ":0", xk + ":1", f"mva:{pp}"], writes=[f"xn_st:{pp}"])
        sm_rel[("mva", pp)] = tk(t4, t6)
        sm_rel[("lnv", pp)] = tk(t5)
        sm_rel[("inv", pp)] = tk(t6)
        sm_rel[("mv6", pp)] = tk(t3)
        return t6

    def ln_transposes(b, t6):
        pp = b % 2
        evs = []
        for h4 in range(2):
            evs.append(transpose_burst(
                [xn_st[:, pp, 128 * k:128 * (k + 1)]
                 for k in range(4 * h4, 4 * h4 + 4)],
                xnT[:, 4 * h4:4 * h4 + 4, b, :], tk(t6),
                wkey=f"xnT:{b}:{h4}"))
        sm_rel[("xn_st", pp)] = tk(*evs, (PEs, PEs.count))
        return tk(*evs)

    def ln_block(b, xin_ticket, phase):
        return ln_transposes(b, ln_stats(b, xin_ticket))

    def router_block(b, xnT_t):
        pp = b % 2
        wl = tk(xnT_t, bank_rel[5], t_rw, t_rb, sm_rel.get(("logit", pp)))
        tm = None
        for k in range(KC):
            tm = sch.emit("tensor",
                          lambda t, kk=k: t.matmul(bank(5, 0, NE),
                                                   xnT[:, kk, b, :], rw_sb[:, kk, :],
                                                   start=(kk == 0), stop=(kk == KC - 1)),
                          wl if k == 0 else None, PEs if k == KC - 1 else None, 1)
        t1 = sch.emit("vector",
                      lambda v: v.tensor_tensor(logit[:, pp, :], bank(5, 0, NE),
                                                rb_sb[:, :], op=ALU.add),
                      tk(tm), DVs, 1, reads=["ps5", "rb_sb"], writes=[f"logit:{pp}"])
        bank_rel[5] = tk(t1)
        t2 = sch.emit("vector",
                      lambda v: v.reduce_max(mxb[:, pp, :], logit[:, pp, :], axis=AX.X),
                      tk(sm_rel.get(("mxb", pp))), DVs, 1,
                      reads=[f"logit:{pp}"], writes=[f"mxb:{pp}"])
        t3 = sch.emit("vector",
                      lambda v: v.reduce_max(nmx[:, pp, :], logit[:, pp, :], axis=AX.X,
                                             negate=True),
                      tk(sm_rel.get(("nmx", pp))), DVs, 1,
                      reads=[f"logit:{pp}"], writes=[f"nmx:{pp}"])
        t4 = sch.emit("scalar",
                      lambda sc: sc.activation(probs[:, pp, :], logit[:, pp, :],
                                               AF.Exp, bias=nmx[:, pp, :], scale=1.0,
                                               accum_out=sume[:, pp, :]),
                      tk(t1, t3, sm_rel.get(("probs", pp)), sm_rel.get(("sume", pp))),
                      ACs, 1, reads=[f"alogit:{pp}"], writes=[f"probs:{pp}", f"sume:{pp}"])
        t5 = sch.emit("vector",
                      lambda v: v.reciprocal(gate[:, pp, :], sume[:, pp, :]),
                      tk(t4, sm_rel.get(("gate", pp))), DVs, 1,
                      reads=[f"sume:{pp}"], writes=[f"gate:{pp}"])
        t6 = sch.emit("vector",
                      lambda v: v.tensor_scalar(probs[:, pp, :], logit[:, pp, :],
                                                mxb[:, pp, 0:1], None, op0=ALU.is_equal),
                      tk(t4), DVs, 1,
                      reads=[f"logit:{pp}", f"mxb:{pp}"], writes=[f"probs:{pp}"])
        t7 = sch.emit("vector",
                      lambda v: v.tensor_scalar(pcoef[:, b, :], probs[:, pp, :],
                                                gate[:, pp, :], 0.5,
                                                op0=ALU.mult, op1=ALU.mult),
                      tk(t6), DVs, 1,
                      reads=[f"probs:{pp}", f"gate:{pp}"], writes=[f"pc:{b}"])
        sm_rel[("logit", pp)] = tk(t4, t6)
        sm_rel[("mxb", pp)] = tk(t6)
        sm_rel[("nmx", pp)] = tk(t4)
        sm_rel[("sume", pp)] = tk(t5)
        sm_rel[("gate", pp)] = tk(t7)
        sm_rel[("probs", pp)] = tk(t7)
        t8 = sch.emit("vector",
                      lambda v: v.tensor_copy(pcb[:, b, :], pcoef[:, b, :]),
                      tk(t7, sm_rel.get(("pcb", b))), DVs, 1,
                      reads=[f"pc:{b}"], writes=[f"pcb:{b}"])
        bank_rel[5] = tk(t1)

        def emit_ct():
            bb2 = tr_rot[0] % 2
            tr_rot[0] += 1
            ptc = sch.emit("tensor",
                           lambda t, bb=bb2: t.transpose(
                               PTh[0:8, 1024 * bb:1024 * bb + 128],
                               pcb[:, b, :], ident[:, :]),
                           tk(t8, tbank_rel[bb2], t_ident), PEs, 1)
            if bb2 == 0:
                dtc = sch.emit("vector",
                               lambda v, bb=bb2: v.tensor_copy(
                                   cTt[:, b, :], PTh[0:8, 1024 * bb:1024 * bb + 128]),
                               tk(ptc), DVs, 1, reads=["pt"], writes=[f"cT:{b}"])
            else:
                dtc = sch.emit("scalar",
                               lambda sc, bb=bb2: sc.copy(
                                   cTt[:, b, :], PTh[0:8, 1024 * bb:1024 * bb + 128]),
                               tk(ptc), ACs, 1, reads=["apt"], writes=[f"acT:{b}"])
            tbank_rel[bb2] = tk(dtc)
            sm_rel[("pcb", b)] = tk((PEs, PEs.count))
            return tk(dtc)
        return tk(t7), emit_ct


    ffn_rot = [0]

    def ffn_pass(b, e, parity, first, xnT_t, pc_t, ct_t, wexp_t, weave=None):
        evs = []
        ct_ticket = None
        if first:
            ct_ticket = ct_t() if callable(ct_t) else ct_t
        for fh in range(2):
            bk = ffn_rot[0] % 4
            ffn_rot[0] += 1
            wl = tk(bank_rel[bk], xnT_t, wexp_t)
            tm = None
            for k in range(KC):
                tm = sch.emit("tensor",
                              lambda t, kk=k, bb=bk, ff=fh: t.matmul(
                                  bank(bb), xnT[:, kk, b, :],
                                  wexp[:, parity, kk, 512 * ff:512 * (ff + 1)],
                                  start=(kk == 0), stop=(kk == KC - 1)),
                              wl if k == 0 else None,
                              PEs if k == KC - 1 else None, 1)
            if bk < 1:
                ev = sch.emit("vector",
                              lambda v, bb=bk, ff=fh: v.scalar_tensor_tensor(
                                  x_sb[:, b, 512 * ff:512 * (ff + 1)],
                                  bank(bb), pcoef[:, b, e:e + 1],
                                  x_sb[:, b, 512 * ff:512 * (ff + 1)],
                                  op0=ALU.mult, op1=ALU.add),
                              tk(tm, pc_t), DVs, 1,
                              reads=[f"ps{bk}", f"pc:{b}", f"x:{b}:{fh}"],
                              writes=[f"x:{b}:{fh}"])
                bank_rel[bk] = tk(ev)
            else:
                sl = bk - 1
                ta = sch.emit("scalar",
                              lambda sc, bb=bk, ss=sl: sc.activation(
                                  ffn_tmp[:, ss, :], bank(bb), AF.Copy, bias=0.0,
                                  scale=pcoef[:, b, e:e + 1]),
                              tk(tm, pc_t, sm_rel.get(("ffn_tmp", sl))), ACs, 1,
                              reads=[f"aps{bk}"], writes=[f"ffntmp:{ss}" if False else f"ffntmp:{sl}"])
                bank_rel[bk] = tk(ta)
                ev = sch.emit("vector",
                              lambda v, ss=sl, ff=fh: v.tensor_tensor(
                                  x_sb[:, b, 512 * ff:512 * (ff + 1)],
                                  x_sb[:, b, 512 * ff:512 * (ff + 1)],
                                  ffn_tmp[:, ss, :], op=ALU.add),
                              tk(ta), DVs, 1,
                              reads=[f"x:{b}:{fh}", f"ftmp:{sl}"],
                              writes=[f"x:{b}:{fh}"])
                sm_rel[("ffn_tmp", sl)] = tk(ev)
            evs.append(ev)
            if weave is not None:
                weave(fh)
        if first:
            for fh in range(2):
                bk = 4 + fh
                wl = tk(bank_rel[bk], ct_ticket, t_eb)
                tb = sch.emit("tensor",
                              lambda t, ff=fh, bb=bk: t.matmul(
                                  bank(bb), cTt[:, b, :],
                                  eb_sb[:, 512 * ff:512 * (ff + 1)],
                                  start=True, stop=True),
                              wl, PEs, 1)
                ev = sch.emit("vector",
                              lambda v, ff=fh, bb=bk: v.tensor_tensor(
                                  x_sb[:, b, 512 * ff:512 * (ff + 1)],
                                  x_sb[:, b, 512 * ff:512 * (ff + 1)],
                                  bank(bb), op=ALU.add),
                              tk(tb), DVs, 1,
                              reads=[f"ps{bk}", f"x:{b}:{fh}"], writes=[f"x:{b}:{fh}"])
                bank_rel[bk] = tk(ev)
                evs.append(ev)
        return tk(*evs)

    def store_block(s, b, ticket):
        pp = b % 2
        tc = sch.emit("scalar",
                      lambda sc: sc.copy(out_st[:, pp, :], x_sb[:, b, :]),
                      tk(ticket, outst_rel[pp]), ACs, 1,
                      reads=[f"ax:{b}"], writes=[f"outst:{pp}"])
        blk0 = (s * n_blk + b) * 128
        st = dma_s(y_d[blk0:blk0 + 128, :], out_st[:, pp, :], tk(tc))
        outst_rel[pp] = st
        return st

    n_ffn_phases = (1 if "A" in phases else 0) + (1 if "C" in phases else 0)
    total_passes = n_seq * n_ffn_phases * NE
    loads_done = [0]
    load_t = [None] * (total_passes + 1)

    def ensure_loads(upto):
        while loads_done[0] <= min(upto, total_passes - 1):
            gp = loads_done[0]
            e = gp % NE
            par = gp % 2
            load_t[gp] = dma_g(wexp[:, par, :, :], ew_d[e],
                               tk(wexp_rel[par]))
            loads_done[0] += 1

    gpass = [0]
    ln_tr_acc = []

    x1_done = [None] * n_blk
    x2_done = [None] * n_blk
    last_phase = phases[-1]

    for s in range(n_seq):
        x_t = [dma_g(x_sb[:, b, :],
                     x_d[(s * n_blk + b) * 128:(s * n_blk + b) * 128 + 128, :],
                     tk(xslot_rel[b]))
               for b in range(n_blk)]
        if s == 0:
            load_wproj()
            if "B" in phases:
                build_qkv_cache()

        xnT_t = [None] * n_blk
        xn3_t = [None] * n_blk
        pc_t = [None] * n_blk
        ct_t = [None] * n_blk

        if "A" in phases:
            ln_a = [None] * n_blk
            for p in range(NE):
                gp = gpass[0]
                ensure_loads(gp + 1)
                par = gp % 2
                if p == 0:
                    ln_a[0] = ln_stats(0, tk(x_t[0]))
                    if n_blk > 1:
                        ln_a[1] = ln_stats(1, tk(x_t[1]))
                    xnT_t[0] = ln_transposes(0, ln_a[0])
                    pc_t[0], ct_t[0] = router_block(0, xnT_t[0])
                for b in range(n_blk):
                    weave = None
                    if p == 0 and b + 1 < n_blk:
                        def weave(fh, bb1=b + 1):
                            pp1 = bb1 % 2
                            ln_tr_acc.append(transpose_burst(
                                [xn_st[:, pp1, 128 * k:128 * (k + 1)]
                                 for k in range(4 * fh, 4 * fh + 4)],
                                xnT[:, 4 * fh:4 * fh + 4, bb1, :], tk(ln_a[bb1]),
                                wkey=f"xnT:{bb1}:{fh}"))
                            if fh == 1:
                                sm_rel[("xn_st", pp1)] = tk(*ln_tr_acc,
                                                            (PEs, PEs.count))
                                xnT_t[bb1] = tk(*ln_tr_acc)
                                ln_tr_acc.clear()
                                pc_t[bb1], ct_t[bb1] = router_block(bb1, xnT_t[bb1])
                                if bb1 + 1 < n_blk:
                                    ln_a[bb1 + 1] = ln_stats(bb1 + 1, tk(x_t[bb1 + 1]))
                    ev = ffn_pass(b, p, par, p == 0, xnT_t[b], pc_t[b], ct_t[b],
                                  load_t[gp], weave=weave)
                    if p == NE - 1:
                        x1_done[b] = ev
                        if last_phase == "A":
                            xslot_rel[b] = tk(store_block(s, b, ev))
                wexp_rel[par] = tk((PEs, PEs.count))
                gpass[0] += 1
        else:
            for b in range(n_blk):
                x1_done[b] = tk(x_t[b])

        if "B" in phases:
            qkv_evt = [None] * n_blk
            ln2_t = [None] * n_blk
            ln2_s = [None] * n_blk
            qload_i = [0]

            def qkv_group(g):
                blks = [2 * g, 2 * g + 1]
                for b in blks:
                    if ln2_s[b] is None:
                        ln2_s[b] = ln_stats(b, x1_done[b])
                for b in blks:
                    ln2_t[b] = ln_transposes(b, ln2_s[b])
                all_ev = []
                for jb in range(6):
                    sl = qload_i[0] % 2
                    qload_i[0] += 1
                    fbase = 4 * jb
                    wt = dma_s(wqkv_sb[:, sl, :, :, :],
                               wqc_d[:, fbase:fbase + 4, :, :],
                               tk(wq_rel[sl], qc_box[0]))
                    for fi in range(4):
                        f = fbase + fi
                        bk = f % 2
                        wl = tk(bank_rel[bk], wt, ln2_t[blks[0]], ln2_t[blks[1]])
                        tm = None
                        for k in range(KC):
                            tm = sch.emit("tensor",
                                          lambda t, kk=k, bb=bk, ss=sl, ii=fi: t.matmul(
                                              bank(bb, 0, 256),
                                              wqkv_sb[:, ss, ii, kk, :],
                                              xnT[:, kk, 2 * g:2 * g + 2, :],
                                              start=(kk == 0), stop=(kk == KC - 1)),
                                          wl if k == 0 else None,
                                          PEs if k == KC - 1 else None, 1)
                        wq_rel[sl] = tk(wq_rel[sl], tm)
                        if f < 8:
                            eng, sem = (("vector", DVs) if bk == 0
                                        else ("scalar", ACs))
                            sA = (2 * g) % R_QT
                            def _cp(d_, s_, dve=(bk == 0)):
                                def f_(e2, d=d_, ss=s_, dv_=dve):
                                    return (e2.tensor_copy(d, ss) if dv_
                                            else e2.copy(d, ss))
                                return f_
                            ev1 = sch.emit(eng, _cp(qT[:, sA:sA + 2, f, :],
                                                    bank(bk, 0, 256)),
                                           tk(tm, qt_rel[sA], qt_rel[(sA + 1) % R_QT]),
                                           sem, 1,
                                           reads=[f"ps{bk}"],
                                           writes=[f"qT:{2*g}", f"qT:{2*g+1}"])
                            bank_rel[bk] = tk(ev1)
                            all_ev += [ev1]
                        elif f < 16:
                            eng, sem = (("vector", DVs) if bk == 0
                                        else ("scalar", ACs))
                            sK = (2 * g) % R_KT
                            def _cp2(d_, s_, dve=(bk == 0)):
                                def f_(e2, d=d_, ss=s_, dv_=dve):
                                    return (e2.tensor_copy(d, ss) if dv_
                                            else e2.copy(d, ss))
                                return f_
                            ev1 = sch.emit(eng, _cp2(kT[:, sK:sK + 2, f - 8, :],
                                                     bank(bk, 0, 256)),
                                           tk(tm, kt_rel[sK], kt_rel[(sK + 1) % R_KT]),
                                           sem, 1,
                                           reads=[f"ps{bk}"],
                                           writes=[f"kT:{2*g}", f"kT:{2*g+1}"])
                            bank_rel[bk] = tk(ev1)
                            all_ev += [ev1]
                        else:
                            fv = f - 16
                            if bk == 0:
                                ev = sch.emit("vector",
                                              lambda v, bb=bk, ff=fv: v.tensor_copy(
                                                  vfeat[:, ff % 2, :], bank(bb, 0, 256)),
                                              tk(tm, sm_rel.get(("vfeat", fv % 2))), DVs, 1,
                                              reads=[f"ps{bk}"], writes=[f"vf:{fv%2}"])
                            else:
                                ev = sch.emit("scalar",
                                              lambda sc, bb=bk, ff=fv: sc.copy(
                                                  vfeat[:, ff % 2, :], bank(bb, 0, 256)),
                                              tk(tm, sm_rel.get(("vfeat", fv % 2))), ACs, 1,
                                              reads=[f"aps{bk}"], writes=[f"avf:{fv%2}"])
                            bank_rel[bk] = tk(ev)
                            evs_v = transpose_burst_multi(
                                [(vfeat[:, fv % 2, 128 * half:128 * (half + 1)],
                                  vtok[:, b2 % R_V, 2 * fv:2 * fv + 2, 0:64])
                                 for half, b2 in enumerate(blks)],
                                tk(ev, v_rel[blks[0] % R_V], v_rel[blks[1] % R_V],
                                   t_vones),
                                wkey=f"v:{2*g}")
                            all_ev += evs_v
                            sm_rel[("vfeat", fv % 2)] = tk((PEs, PEs.count))
                return tk(*all_ev)

            def attn_block(b, weave=None):
                if b_stop == 0:
                    qt_rel[b % R_QT] = tk((PEs, PEs.count))
                    if b >= 1:
                        kt_rel[(b - 1) % R_KT] = tk((PEs, PEs.count))
                        v_rel[(b - 1) % R_V] = tk((PEs, PEs.count))
                    if b == n_blk - 1:
                        kt_rel[b % R_KT] = tk((PEs, PEs.count))
                        v_rel[b % R_V] = tk((PEs, PEs.count))
                    return tk(qkv_evt[b], x1_done[b])
                chunks = [c for c in range(3) if 0 <= b + c - 1 < n_blk]
                nch = len(chunks)
                pp = b % 2
                exp_t = [None] * H
                eslot = [0] * H
                wl_qkv = tk(qkv_evt[b],
                            qkv_evt[b + 1] if b + 1 < n_blk else None,
                            qkv_evt[b - 1] if b - 1 >= 0 else None)
                x2e = []
                for half in range(2):
                    heads = range(8 * half, 8 * half + 8)
                    for pi in range(4):
                        gpair = 4 * half + pi
                        bkA = 4 if gpair % 2 == 0 else 2
                        pslot = (b * 8 + gpair) % (R_EKM // 2)
                        tmp_ = [None, None]
                        for sub in range(2):
                            h = 8 * half + 2 * pi + sub
                            eslot[h] = 2 * pslot + sub
                            fc = h // 2
                            bk = bkA + sub
                            part = slice(64 * sub, 64 * sub + 64)
                            wl = tk(bank_rel[bk], wl_qkv)
                            tm = None
                            for i, c in enumerate(chunks):
                                kb = b + c - 1
                                tm = sch.emit("tensor",
                                              lambda t, ii=i, hh=h, bb=bk, ps=part,
                                              kk=kb, ff=fc: t.matmul(
                                                  bank(bb, 128 * ii, 128),
                                                  kT[ps, kk % R_KT, ff, :],
                                                  qT[ps, b % R_QT, ff, :],
                                                  start=True, stop=True),
                                              wl if i == 0 else None,
                                              PEs if i == nch - 1 else None, 1)
                            tmp_[sub] = tm
                        te0 = sch.emit("scalar",
                                       lambda sc, bb=bkA, ss=2 * pslot, nn=nch: sc.activation(
                                           ekm[:, ss, 0:128 * nn], bank(bb, 0, 128 * nn),
                                           AF.Exp, bias=zc[:, :], scale=0.125),
                                       tk(tmp_[0], ekm_rel[2 * pslot], t_cst3), ACs, 1,
                                       reads=[f"aps{bkA}"], writes=[f"ekm:{2*pslot}"])
                        te = sch.emit("scalar",
                                      lambda sc, bb=bkA + 1, ss=2 * pslot + 1, nn=nch: sc.activation(
                                          ekm[:, ss, 0:128 * nn], bank(bb, 0, 128 * nn),
                                          AF.Exp, bias=zc[:, :], scale=0.125),
                                      tk(tmp_[1], ekm_rel[2 * pslot + 1], t_cst3), ACs, 1,
                                      reads=[f"aps{bkA+1}"], writes=[f"ekm:{2*pslot+1}"])
                        bank_rel[bkA] = tk(te0)
                        bank_rel[bkA + 1] = tk(te)
                        te_by_sub = [te0, te]
                        for sub in range(2):
                            h = 8 * half + 2 * pi + sub
                            ss2 = 2 * pslot + sub
                            te = te_by_sub[sub]
                            mt = [None] * nch
                            for i, c in enumerate(chunks):
                                if c == 1:
                                    continue
                                m = mlo if c == 0 else mhi
                                mt[i] = sch.emit(
                                    "vector",
                                    lambda v, s2=ss2, ii=i, mm=m: v.tensor_tensor(
                                        ekm[:, s2, 128 * ii:128 * (ii + 1)],
                                        ekm[:, s2, 128 * ii:128 * (ii + 1)],
                                        mm[:, :], op=ALU.mult),
                                    tk(te, t_mlo, t_mhi), DVs, 1,
                                    reads=[f"ekm:{ss2}"], writes=[f"ekm:{ss2}"])
                            exp_t[h] = (tk(te), [tk(te) if x is None else tk(x)
                                                 for x in mt])
                    if b_stop == 1:
                        continue
                    tmh = {}
                    for h in heads:
                        bk = 2 + ((h % 8) // 4)
                        off = 128 * (h % 4)
                        te_t, mt = exp_t[h]
                        order = sorted(range(nch), key=lambda i: chunks[i] != 1)
                        base = tk(bank_rel[bk], te_t,
                                  *[v_rel[(b + c - 1) % R_V] for c in chunks])
                        tm = None
                        for j, i in enumerate(order):
                            kb = b + chunks[i] - 1
                            wl = tk(base, mt[i]) if j == 0 else tk(mt[i])
                            tm = sch.emit("tensor",
                                          lambda t, hh=h, ii=i, bb=bk, oo=off,
                                          kk=kb, s0=(j == 0), s1=(j == nch - 1): t.matmul(
                                              bank(bb, oo, 65),
                                              ekm[:, eslot[hh], 128 * ii:128 * (ii + 1)],
                                              vtok[:, kk % R_V, hh, :],
                                              start=s0, stop=s1),
                                          wl,
                                          PEs if j == nch - 1 else None, 1)
                        tmh[h] = tm
                    for h in heads:
                        ekm_rel[eslot[h]] = tk(tmh[h])
                    if weave is not None:
                        weave(half)
                    rec_t = {}
                    for jb, bk in enumerate((2, 3)):
                        h0 = 8 * half + 4 * jb
                        wl = tk(*[tmh[h] for h in range(h0, h0 + 4)],
                                sm_rel.get(("invl", pp, jb)))
                        rec = sch.emit("vector",
                                       lambda v, bb=bk, hh=h0: v.reciprocal(
                                           invl[:, pp, hh:hh + 4],
                                           bass.AP(PSh, 512 * bb + 64, [[3072, 128], [128, 4]])),
                                       wl, DVs, 1,
                                       reads=[f"ps{bk}"], writes=[f"invl:{pp}:{jb}"])
                        rec_t[bk] = rec
                    evh = []
                    for h in heads:
                        bk = 2 + ((h % 8) // 4)
                        off = 128 * (h % 4)
                        rt = rec_t[bk]
                        if bk == 2:
                            ev = sch.emit("vector",
                                          lambda v, hh=h, bb=bk, oo=off: v.tensor_scalar(
                                              anorm[:, pp, 64 * hh:64 * (hh + 1)],
                                              bank(bb, oo, 64),
                                              invl[:, pp, hh:hh + 1], None, op0=ALU.mult),
                                          tk(rt, sm_rel.get(("anorm", pp))), DVs, 1,
                                          reads=[f"ps{bk}", f"invl:{pp}:0"],
                                          writes=[f"anorm:{pp}:{h}"])
                        else:
                            ev = sch.emit("scalar",
                                          lambda sc, hh=h, bb=bk, oo=off: sc.activation(
                                              anorm[:, pp, 64 * hh:64 * (hh + 1)],
                                              bank(bb, oo, 64), AF.Copy, bias=0.0,
                                              scale=invl[:, pp, hh:hh + 1]),
                                          tk(rt, sm_rel.get(("anorm", pp))), ACs, 1,
                                          reads=[f"aps{bk}", f"ainvl:{pp}"],
                                          writes=[f"anorm:{pp}:{h}"])
                        evh.append(ev)
                        bank_rel[bk] = tk(bank_rel[bk], ev)
                    x2e += evh
                if b_stop == 1:
                    ee = tk(*[exp_t[h] for h in range(H)], x1_done[b], qkv_evt[b])
                    qt_rel[b % R_QT] = tk((PEs, PEs.count))
                    if b >= 1:
                        kt_rel[(b - 1) % R_KT] = tk((PEs, PEs.count))
                        v_rel[(b - 1) % R_V] = tk((PEs, PEs.count))
                    if b == n_blk - 1:
                        kt_rel[b % R_KT] = tk((PEs, PEs.count))
                        v_rel[b % R_V] = tk((PEs, PEs.count))
                    for h in range(H):
                        ekm_rel[eslot[h]] = ee
                    return ee
                sm_rel[("invl", pp, 0)] = tk(*x2e)
                sm_rel[("invl", pp, 1)] = tk(*x2e)
                if b_stop == 2:
                    ee = tk(*x2e, x1_done[b])
                    qt_rel[b % R_QT] = tk((PEs, PEs.count))
                    if b >= 1:
                        kt_rel[(b - 1) % R_KT] = tk((PEs, PEs.count))
                        v_rel[(b - 1) % R_V] = tk((PEs, PEs.count))
                    if b == n_blk - 1:
                        kt_rel[b % R_KT] = tk((PEs, PEs.count))
                        v_rel[b % R_V] = tk((PEs, PEs.count))
                    return ee
                aT_l = []
                for h4 in range(2):
                    aT_l.append(transpose_burst(
                        [anorm[:, pp, 128 * k:128 * (k + 1)]
                         for k in range(4 * h4, 4 * h4 + 4)],
                        aT[:, pp, 4 * h4:4 * h4 + 4, :], tk(*x2e),
                        wkey=f"aT:{pp}:{h4}"))
                aT_t = tk(*aT_l)
                sm_rel[("anorm", pp)] = tk(aT_t, (PEs, PEs.count))
                pevs = []
                for fh in range(2):
                    bk = fh
                    wl = tk(bank_rel[bk], aT_t, t_wproj_box[0], sm_rel.get(("aTr", pp)))
                    tm = None
                    for k in range(KC):
                        tm = sch.emit("tensor",
                                      lambda t, kk=k, bb=bk, ff=fh: t.matmul(
                                          bank(bb), aT[:, pp, kk, :],
                                          wproj_sb[:, kk, 512 * ff:512 * (ff + 1)],
                                          start=(kk == 0), stop=(kk == KC - 1)),
                                      wl if k == 0 else None,
                                      PEs if k == KC - 1 else None, 1)
                    ev = sch.emit("vector",
                                  lambda v, bb=bk, ff=fh: v.tensor_tensor(
                                      x_sb[:, b, 512 * ff:512 * (ff + 1)],
                                      x_sb[:, b, 512 * ff:512 * (ff + 1)],
                                      bank(bb), op=ALU.add),
                                  tk(tm), DVs, 1,
                                  reads=[f"ps{bk}", f"x:{b}:{fh}"], writes=[f"x:{b}:{fh}"])
                    bank_rel[bk] = tk(ev)
                    pevs.append(ev)
                sm_rel[("aTr", pp)] = tk((PEs, PEs.count))
                tbp = sch.emit("vector",
                               lambda v: v.tensor_tensor(x_sb[:, b, :], x_sb[:, b, :],
                                                         bproj_sb[:, :], op=ALU.add),
                               tk(t_bp, *pevs), DVs, 1,
                               reads=["bproj_sb", f"x:{b}:0", f"x:{b}:1"], writes=[f"x:{b}:0", f"x:{b}:1"])
                qt_rel[b % R_QT] = tk(tbp, (PEs, PEs.count))
                if b >= 1:
                    kt_rel[(b - 1) % R_KT] = tk((PEs, PEs.count))
                    v_rel[(b - 1) % R_V] = tk((PEs, PEs.count))
                if b == n_blk - 1:
                    kt_rel[b % R_KT] = tk((PEs, PEs.count))
                    v_rel[b % R_V] = tk((PEs, PEs.count))
                return tk(tbp)

            attn_next = 0
            for g in range(n_blk // 2):
                gt = qkv_group(g)
                qkv_evt[2 * g] = gt
                qkv_evt[2 * g + 1] = gt
                while attn_next < n_blk and (
                        attn_next + 1 >= n_blk or qkv_evt[attn_next + 1] is not None):
                    bb_ = attn_next
                    weave = None
                    if last_phase != "B" and "C" in phases and bb_ >= 1:
                        bprev = bb_ - 1
                        st3 = ln_stats(bprev, x2_done[bprev])
                        tr3 = []

                        def weave(half, bp=bprev, st=st3, trl=tr3):
                            pp3 = bp % 2
                            trl.append(transpose_burst(
                                [xn_st[:, pp3, 128 * k:128 * (k + 1)]
                                 for k in range(4 * half, 4 * half + 4)],
                                xnT[:, 4 * half:4 * half + 4, bp, :], tk(st),
                                wkey=f"xnT:{bp}:{half}"))
                            if half == 1:
                                sm_rel[("xn_st", pp3)] = tk(*trl, (PEs, PEs.count))
                                xn3_t[bp] = tk(*trl)
                                pc_t[bp], ct_t[bp] = router_block(bp, xn3_t[bp])
                    x2_done[bb_] = attn_block(bb_, weave=weave)
                    if last_phase == "B":
                        xslot_rel[bb_] = tk(store_block(s, bb_, x2_done[bb_]))
                    attn_next += 1
            if last_phase != "B" and "C" in phases and n_blk >= 1:
                bl = n_blk - 1
                xn3_t[bl] = ln_block(bl, x2_done[bl], "C")
                pc_t[bl], ct_t[bl] = router_block(bl, xn3_t[bl])
        else:
            for b in range(n_blk):
                x2_done[b] = x1_done[b]

        if "C" in phases:
            p_start = DBG_P_START if ("B" in phases and last_phase != "B" and not DBG_NO_C0) else 0
            for p in range(p_start, NE):
                gp = gpass[0]
                ensure_loads(gp + 1)
                par = gp % 2
                for b in range(n_blk):
                    if p == p_start and xn3_t[b] is None:
                        xn3_t[b] = ln_block(b, x2_done[b], "C")
                        pc_t[b], ct_t[b] = router_block(b, xn3_t[b])
                    ev = ffn_pass(b, p, par, (p == 0) or (p == 1 and not DBG_C0_BIAS
                                               and p_start == 1), xn3_t[b], pc_t[b],
                                  ct_t[b], load_t[gp])
                    if p == NE - 1:
                        xslot_rel[b] = tk(store_block(s, b, ev))
                wexp_rel[par] = tk((PEs, PEs.count))
                gpass[0] += 1

        if s == 0:
            for nm, shp in taps:
                assert shp == [n_blk * 128, DM]
                for b in range(n_blk):
                    dma_g(tap_d[nm][b * 128:(b + 1) * 128, :], x_sb[:, b, :],
                          tk(xslot_rel[b] if last_phase == "C" else x2_done[b]))

    if DBG_DUMP_PC:
        dbg_pc = nc.dram_tensor("dbg_pc", [n_blk * 128, NE], F32, kind="ExternalOutput").ap()
        dbg_ct = nc.dram_tensor("dbg_ct", [NE * n_blk, 128], F32, kind="ExternalOutput").ap()
        for b in range(n_blk):
            dma_g(dbg_pc[b * 128:(b + 1) * 128, :], pcoef[:, b, :],
                  tk((PEs, PEs.count), (DVs, DVs.count)))
        dma_g(dbg_ct.rearrange("(e b) f -> e b f", e=NE), cTt[:, :, :],
              tk((PEs, PEs.count), (DVs, DVs.count), (ACs, ACs.count)))
        dbg_x = nc.dram_tensor("dbg_x", [n_blk * 128, DM], F32, kind="ExternalOutput").ap()
        for b in range(n_blk):
            dma_g(dbg_x[b * 128:(b + 1) * 128, :], x_sb[:, b, :],
                  tk((PEs, PEs.count), (DVs, DVs.count), (ACs, ACs.count)))
        dbg_xnt = nc.dram_tensor("dbg_xnt", [128, KC, n_blk, 128], F32, kind="ExternalOutput").ap()
        dma_g(dbg_xnt, xnT[:, :, :, :], tk((PEs, PEs.count), (DVs, DVs.count), (ACs, ACs.count)))

    sch.emit("sync", lambda g: g.nop(),
             tk(*[(l, l.count) for l in DS_l], *[(l, l.count) for l in DG_l]), None)

    for sem in sch.sems.values():
        cm = nc.semaphore(sem.name)
        sem.handle = cm.__enter__()
        ctx.append(cm)
    with nc.Block() as block:
        block.tensor(lambda t: sch.replay_engine("tensor", t))
        block.vector(lambda v: sch.replay_engine("vector", v))
        block.scalar(lambda sc: sch.replay_engine("scalar", sc))
        block.gpsimd(lambda g: sch.replay_engine("gpsimd", g))
        block.sync(lambda sy: sch.replay_engine("sync", sy))

    for t in reversed(ctx):
        t.__exit__(None, None, None)
    assert sbuf_bytes[0] <= 207 * 1024, f"SBUF over budget: {sbuf_bytes[0]}"
    build_stats = dict(steps=len(sch.steps), drains=sch.n_drains,
                       sbuf=sbuf_bytes[0])
    return nc, build_stats


def _pm_kf(w):
    f = w.shape[1]
    return np.ascontiguousarray(w.reshape(KC, 128, f).transpose(1, 0, 2))


def _pm_qkv(w):
    return np.ascontiguousarray(
        w.reshape(KC, 128, 24, 128).transpose(1, 2, 0, 3))


def _pm_ew(w):
    return np.ascontiguousarray(
        w.reshape(NE, KC, 128, DM).transpose(0, 2, 1, 3).reshape(NE, 128, KC * DM))


def _consts():
    import ml_dtypes
    bfd = ml_dtypes.bfloat16
    ident = np.eye(128, dtype=np.float32).astype(bfd)
    r = np.arange(128)
    mlo = (r[:, None] >= r[None, :]).astype(np.float32).astype(bfd)
    mhi = (r[:, None] <= r[None, :]).astype(np.float32).astype(bfd)
    return ident, mlo, mhi


def make_in_map(x_shard, w):
    ident, mlo, mhi = _consts()
    return {
        "x": np.ascontiguousarray(x_shard.reshape(-1, DM)),
        "wqkv": w["wqkv"], "wproj": w["wproj"], "bproj": w["bproj"],
        "rw": w["rw"], "rb": w["rb"], "ew": w["ew"], "eb": w["eb"],
        "identd": ident, "mlod": mlo, "mhid": mhi,
    }


def kernel(L0_attn_x, L0_attn_w_qkv, L0_attn_w_proj, L0_attn_b_proj,
           L0_ffn_router_w, L0_ffn_router_b, L0_ffn_expert_w, L0_ffn_expert_b):
    x = np.asarray(L0_attn_x, dtype=np.float32)
    w = {
        "wqkv": _pm_qkv(np.asarray(L0_attn_w_qkv, dtype=np.float32)),
        "wproj": _pm_kf(np.asarray(L0_attn_w_proj, dtype=np.float32)),
        "bproj": np.ascontiguousarray(L0_attn_b_proj, dtype=np.float32).reshape(1, DM),
        "rw": _pm_kf(np.asarray(L0_ffn_router_w, dtype=np.float32)),
        "rb": np.ascontiguousarray(L0_ffn_router_b, dtype=np.float32).reshape(1, NE),
        "ew": _pm_ew(np.asarray(L0_ffn_expert_w, dtype=np.float32)),
        "eb": np.ascontiguousarray(L0_ffn_expert_b, dtype=np.float32),
    }
    nc, _ = build_nc()
    in_maps = [make_in_map(x[SEQ_PER_CORE * c:SEQ_PER_CORE * (c + 1)], w)
               for c in range(NCORE)]
    res = run_bass_kernel_spmd(nc, in_maps, core_ids=list(range(NCORE)))
    out = np.concatenate(
        [np.asarray(r["y"]).reshape(SEQ_PER_CORE, S, DM) for r in res.results],
        axis=0)
    return out.astype(np.float32)


# version 23
# speedup vs baseline: 1.0538x; 1.0019x over previous
import numpy as np
import concourse.bass as bass
import concourse.mybir as mybir
from concourse.bass_utils import run_bass_kernel_spmd

F32 = mybir.dt.float32
BF16 = mybir.dt.bfloat16
AF = mybir.ActivationFunctionType
ALU = mybir.AluOpType
AX = mybir.AxisListType

B_TOT, S, DM = 32, 2048, 1024
DBG_C0_INLOOP = False
DBG_C0_BIAS = True
DBG_NO_C0 = True
DBG_P_START = 1
DBG_DUMP_PC = False
H, DH = 16, 64
NCORE = 8
SEQ_PER_CORE = B_TOT // NCORE
NBLK = S // 128
KC = DM // 128
NE = 8
EPS = 1e-5

R_KT = 6
R_V = 6
R_QT = 4
R_EKM = 12


class Sem:
    __slots__ = ("name", "count", "handle")

    def __init__(self, name):
        self.name = name
        self.count = 0
        self.handle = None


def tk(*parts):
    best = {}
    for p in parts:
        if p is None:
            continue
        if isinstance(p, tuple):
            p = [p]
        for s, v in p:
            if best.get(s, 0) < v:
                best[s] = v
    return list(best.items())


class Sched:
    def __init__(self):
        self.steps = []
        self.sems = {}
        self.n_drains = 0

    def sem(self, name):
        if name not in self.sems:
            self.sems[name] = Sem(name)
        return self.sems[name]

    def emit(self, eng, fn, waits=None, inc=None, n=1, reads=(), writes=()):
        w = tk(waits)
        t = None
        if inc is not None:
            inc.count += n
            t = (inc, inc.count)
        self.steps.append((eng, fn, w, (inc, n) if inc else None,
                           tuple(reads), tuple(writes)))
        return t

    def replay_engine(self, eng_name, eng):
        last_wait = {}
        wpos = {}
        rpos = {}
        idx = 0
        cleared_below = -10**9
        WINDOW = 10
        for en, fn, waits, inc, reads, writes in self.steps:
            if en != eng_name:
                continue
            if eng_name in ("vector", "scalar"):
                idx += 1
                hz = max(
                    [wpos.get(r, -10**9) for r in reads]
                    + [max(wpos.get(w, -10**9), rpos.get(w, -10**9)) for w in writes]
                    + [-10**9])
                if hz >= cleared_below and idx - hz <= WINDOW:
                    eng.drain()
                    self.n_drains += 1
                    cleared_below = idx
                for w in writes:
                    wpos[w] = idx
                for r in reads:
                    rpos[r] = idx
            for sem, val in waits:
                if last_wait.get(sem.name, 0) >= val:
                    continue
                eng.wait_ge(sem.handle, val)
                last_wait[sem.name] = val
            inst = fn(eng)
            if inc is not None:
                sem, nn = inc
                inst.then_inc(sem.handle, nn)


def build_nc(n_seq=SEQ_PER_CORE, n_blk=NBLK, phases=("A", "B", "C"), taps=(), b_stop=3):
    nc = bass.Bass("TRN2", target_bir_lowering=False)
    NT = n_seq * n_blk * 128

    x_d = nc.dram_tensor("x", [NT, DM], F32, kind="ExternalInput").ap()
    y_d = nc.dram_tensor("y", [NT, DM], F32, kind="ExternalOutput").ap()
    wqkv_d = nc.dram_tensor("wqkv", [128, 24, KC, 128], F32, kind="ExternalInput").ap()
    wproj_d = nc.dram_tensor("wproj", [128, KC, DM], F32, kind="ExternalInput").ap()
    bproj_d = nc.dram_tensor("bproj", [1, DM], F32, kind="ExternalInput").ap()
    rw_d = nc.dram_tensor("rw", [128, KC, NE], F32, kind="ExternalInput").ap()
    rb_d = nc.dram_tensor("rb", [1, NE], F32, kind="ExternalInput").ap()
    ew_d = nc.dram_tensor("ew", [NE, 128, KC * DM], F32, kind="ExternalInput").ap()
    eb_d = nc.dram_tensor("eb", [NE, DM], F32, kind="ExternalInput").ap()
    ident_d = nc.dram_tensor("identd", [128, 128], BF16, kind="ExternalInput").ap()
    wqc_d = nc.dram_tensor("wqc", [128, 24, KC, 128], BF16,
                           kind="ExternalOutput").ap()
    mlo_d = nc.dram_tensor("mlod", [128, 128], BF16, kind="ExternalInput").ap()
    mhi_d = nc.dram_tensor("mhid", [128, 128], BF16, kind="ExternalInput").ap()
    tap_d = {nm: nc.dram_tensor("tap_" + nm, shp, F32, kind="ExternalOutput").ap()
             for nm, shp in taps}

    sch = Sched()
    PEs = sch.sem("pe")
    DVs = sch.sem("dv")
    ACs = sch.sem("ac")
    NLANE = 4
    DG_l = [sch.sem(f"dg{i}") for i in range(NLANE)]
    DS_l = [sch.sem(f"ds{i}") for i in range(NLANE)]
    dma_rr = {"g": 0, "s": 0}

    ctx = []
    sbuf_bytes = [0]

    def sb(name, shape, dtype):
        t = nc.sbuf_tensor(name, shape, dtype)
        h = t.__enter__()
        ctx.append(t)
        per_part = int(np.prod(shape[1:])) * (2 if dtype == BF16 else 4)
        sbuf_bytes[0] += per_part
        return h

    x_sb = sb("x_sb", [128, n_blk, DM], BF16)
    xnT = sb("xnT", [128, KC, n_blk, 128], BF16)
    xn_st = sb("xn_st", [128, 2, DM], BF16)
    wexp = sb("wexp", [128, 2, KC, DM], BF16)
    wqkv_sb = sb("wqkv_sb", [128, 2, 4, KC, 128], BF16)
    wproj_sb = sb("wproj_sb", [128, KC, DM], BF16)
    rw_sb = sb("rw_sb", [128, KC, NE], BF16)
    eb_sb = sb("eb_sb", [NE, DM], BF16)
    rb_sb = sb("rb_sb", [128, NE], F32)
    rb_row = sb("rb_row", [1, NE], F32)
    bproj_sb = sb("bproj_sb", [128, DM], BF16)
    onesr = sb("onesr", [1, 128], F32)
    epsc = sb("epsc", [128, 1], F32)
    zc = sb("zc", [128, 1], F32)
    ident = sb("ident", [128, 128], BF16)
    mlo = sb("mlo", [128, 128], BF16)
    mhi = sb("mhi", [128, 128], BF16)
    qT = sb("qT", [128, R_QT, KC, 128], BF16)
    kT = sb("kT", [128, R_KT, KC, 128], BF16)
    vtok = sb("vtok", [128, R_V, H, 65], BF16)
    ekm = sb("ekm", [128, R_EKM, 384], BF16)
    vfeat = sb("vfeat", [128, 2, 256], BF16)
    anorm = sb("anorm", [128, 2, DM], BF16)
    aT = sb("aT", [128, 2, KC, 128], BF16)
    out_st = sb("out_st", [128, 2, DM], F32)
    ffn_tmp = sb("ffn_tmp", [128, 3, 512], BF16)
    mv6 = sb("mv6", [128, 2, 12], F32)
    mva = sb("mva", [128, 2, 2], F32)
    lnv = sb("lnv", [128, 2, 1], F32)
    inv = sb("inv", [128, 2, 1], F32)
    logit = sb("logit", [128, 2, NE], F32)
    mxb = sb("mxb", [128, 2, 1], F32)
    nmx = sb("nmx", [128, 2, 1], F32)
    sume = sb("sume", [128, 2, 1], F32)
    gate = sb("gate", [128, 2, 1], F32)
    probs = sb("probs", [128, 2, NE], F32)
    pcoef = sb("pcoef", [128, n_blk, NE], F32)
    cTt = sb("cTt", [NE, n_blk, 128], BF16)
    invl = sb("invl", [128, 2, H], F32)
    pcb = sb("pcb", [128, n_blk, NE], BF16)


    PS = nc.psum_tensor("PS", [128, 3072], F32)
    PSh = PS.__enter__(); ctx.append(PS)
    PT = nc.psum_tensor("PT", [128, 2048], BF16)
    PTh = PT.__enter__(); ctx.append(PT)

    def bank(i, off=0, width=512):
        assert 0 <= i < 6 and off + width <= 512
        return PSh[:, 512 * i + off: 512 * i + off + width]

    def tbank(i, off=0, width=128):
        assert 0 <= i < 2 and off + width <= 1024
        return PTh[:, 1024 * i + off: 1024 * i + off + width]

    bank_rel = [None] * 6
    tbank_rel = [None] * 2
    xslot_rel = [None] * n_blk
    kt_rel = [None] * R_KT
    v_rel = [None] * R_V
    qt_rel = [None] * R_QT
    ekm_rel = [None] * R_EKM
    wexp_rel = [None, None]
    wq_rel = [None, None]
    outst_rel = [None, None]
    sm_rel = {}

    def _dma(ring, eng, lanes, dst, src, waits):
        i = dma_rr[ring]
        dma_rr[ring] += 1
        lane = lanes[i % NLANE]
        return sch.emit(eng, lambda g, d=dst, s=src: g.dma_start(d, s),
                        tk(waits, (lane, lane.count)), lane, 16)

    def dma_g(dst, src, waits=None, tag=""):
        return _dma("g", "gpsimd", DG_l, dst, src, waits)

    def dma_s(dst, src, waits=None, tag=""):
        return _dma("s", "sync", DS_l, dst, src, waits)

    t_ident = dma_s(ident[:, :], ident_d)
    t_mlo = dma_s(mlo[:, :], mlo_d)
    t_mhi = dma_s(mhi[:, :], mhi_d)
    t_rw = dma_g(rw_sb[:, :, :], rw_d)
    t_eb = dma_g(eb_sb[:, :], eb_d)
    t_rbr = dma_g(rb_row[:, :], rb_d)
    t_bpr = dma_g(out_st[0:1, 0, :], bproj_d)
    t_wproj_box = [None]

    def load_wproj():
        t_wproj_box[0] = dma_g(wproj_sb[:, :, :], wproj_d)
        return t_wproj_box[0]
    t_cst = sch.emit("vector", lambda v: v.memset(onesr[:, :], 1.0), None, DVs, 1,
                     writes=["onesr"])
    t_cst2 = sch.emit("vector", lambda v: v.memset(epsc[:, :], EPS), None, DVs, 1,
                      writes=["epsc"])
    t_cst3 = sch.emit("vector", lambda v: v.memset(zc[:, :], 0.0), None, DVs, 1,
                      writes=["zc"])
    t_vones = sch.emit("vector", lambda v: v.memset(vtok[:, :, :, 64:65], 1.0),
                       None, DVs, 1, writes=["vones"])
    t_b1 = sch.emit("tensor",
                    lambda t: t.matmul(bank(0, 0, NE), onesr[:, :], rb_row[:, :],
                                       start=True, stop=True),
                    tk(t_cst, t_rbr), PEs, 1)
    t_rb = sch.emit("vector", lambda v: v.tensor_copy(rb_sb[:, :], bank(0, 0, NE)),
                    tk(t_b1), DVs, 1, reads=["ps0"], writes=["rb_sb"])
    bank_rel[0] = tk(t_rb)
    t_b2 = sch.emit("tensor",
                    lambda t: t.matmul(bank(1), onesr[:, :], out_st[0:1, 0, 0:512],
                                       start=True, stop=True),
                    tk(t_cst, t_bpr), PEs, 1)
    t_b3 = sch.emit("tensor",
                    lambda t: t.matmul(bank(2), onesr[:, :], out_st[0:1, 0, 512:1024],
                                       start=True, stop=True),
                    tk(t_b2), PEs, 1)
    t_bp1 = sch.emit("vector", lambda v: v.tensor_copy(bproj_sb[:, 0:512], bank(1)),
                     tk(t_b3), DVs, 1, reads=["ps1"], writes=["bproj_sb"])
    t_bp2 = sch.emit("vector", lambda v: v.tensor_copy(bproj_sb[:, 512:1024], bank(2)),
                     tk(t_b3), DVs, 1, reads=["ps2"], writes=["bproj_sb"])
    bank_rel[1] = tk(t_bp1)
    bank_rel[2] = tk(t_bp2)
    t_bp = tk(t_bp1, t_bp2)
    outst_rel[0] = tk(t_b3)


    qc_box = [None]

    def build_qkv_cache():
        qc_store = [None] * 6
        for jb in range(6):
            sl = jb % 2
            tl = dma_g(wqkv_sb[:, sl, :, :, :],
                       wqkv_d[:, 4 * jb:4 * jb + 4, :, :],
                       tk(qc_store[jb - 2] if jb >= 2 else None))
            qc_store[jb] = dma_s(wqc_d[:, 4 * jb:4 * jb + 4, :, :],
                                 wqkv_sb[:, sl, :, :, :], tk(tl))
        qc_box[0] = tk(*qc_store)

    tr_rot = [0]
    trb_rot = [0]

    def transpose_burst(srcs, dst_ap, waits, wkey):
        n = len(srcs)
        bank_i = trb_rot[0] % 2
        trb_rot[0] += 1
        wl = tk(tbank_rel[bank_i], waits, t_ident)
        pt = None
        for j, s_ap in enumerate(srcs):
            pt = sch.emit("tensor",
                          lambda t, ss=s_ap, bb=bank_i, jj=j: t.transpose(
                              PTh[:, 1024 * bb + 128 * jj: 1024 * bb + 128 * (jj + 1)],
                              ss, ident[:, :]),
                          wl if j == 0 else None,
                          PEs if j == n - 1 else None, 1)
        src_ap = PTh[:, 1024 * bank_i: 1024 * bank_i + 128 * n]
        if bank_i == 0:
            ev = sch.emit("vector",
                          lambda v, d=dst_ap, s2=src_ap: v.tensor_copy(d, s2),
                          tk(pt), DVs, 1, reads=["pt"], writes=[wkey])
        else:
            ev = sch.emit("scalar",
                          lambda sc, d=dst_ap, s2=src_ap: sc.copy(d, s2),
                          tk(pt), ACs, 1, reads=["apt"], writes=["a" + wkey])
        tbank_rel[bank_i] = tk(ev)
        return ev

    def transpose_burst_multi(pairs, waits, wkey):
        n = len(pairs)
        bank_i = trb_rot[0] % 2
        trb_rot[0] += 1
        wl = tk(tbank_rel[bank_i], waits, t_ident)
        pt = None
        for j, (s_ap, _) in enumerate(pairs):
            pt = sch.emit("tensor",
                          lambda t, ss=s_ap, bb=bank_i, jj=j: t.transpose(
                              PTh[:, 1024 * bb + 128 * jj: 1024 * bb + 128 * (jj + 1)],
                              ss, ident[:, :]),
                          wl if j == 0 else None,
                          PEs if j == n - 1 else None, 1)
        evs = []
        for j, (_, d_ap) in enumerate(pairs):
            s2 = PTh[:, 1024 * bank_i + 128 * j: 1024 * bank_i + 128 * (j + 1)]
            if bank_i == 0:
                evs.append(sch.emit("vector",
                                    lambda v, d=d_ap, ss=s2: v.tensor_copy(d, ss),
                                    tk(pt), DVs, 1, reads=["pt"],
                                    writes=[f"{wkey}:{j}"]))
            else:
                evs.append(sch.emit("scalar",
                                    lambda sc, d=d_ap, ss=s2: sc.copy(d, ss),
                                    tk(pt), ACs, 1, reads=["apt"],
                                    writes=[f"a{wkey}:{j}"]))
        tbank_rel[bank_i] = tk(*evs)
        return evs

    def transpose128(src_ap, dst_ap, waits, rkey, wkey):
        b = tr_rot[0] % 2
        tr_rot[0] += 1
        pt = sch.emit("tensor",
                      lambda t, s=src_ap, bb=b: t.transpose(tbank(bb), s, ident[:, :]),
                      tk(tbank_rel[b], waits, t_ident), PEs, 1)
        dt_ = sch.emit("vector",
                       lambda v, d=dst_ap, bb=b: v.tensor_copy(d, tbank(bb)),
                       tk(pt), DVs, 1, reads=[rkey], writes=[wkey])
        tbank_rel[b] = tk(dt_)
        return dt_

    def ln_stats(b, xin_ticket):
        pp = b % 2
        xk = f"x:{b}"
        t1 = sch.emit("vector",
                      lambda v: v.bn_stats(mv6[:, pp, 0:6], x_sb[:, b, 0:512]),
                      tk(xin_ticket, sm_rel.get(("mv6", pp))), DVs, 1,
                      reads=[xk + ":0", xk + ":1"], writes=[f"mv6:{pp}"])
        t2 = sch.emit("vector",
                      lambda v: v.bn_stats(mv6[:, pp, 6:12], x_sb[:, b, 512:1024]),
                      None, DVs, 1, reads=[xk + ":0", xk + ":1"], writes=[f"mv6:{pp}"])
        t3 = sch.emit("vector",
                      lambda v: v.bn_aggr(mva[:, pp, :], mv6[:, pp, :]),
                      tk(sm_rel.get(("mva", pp))), DVs, 1,
                      reads=[f"mv6:{pp}"], writes=[f"mva:{pp}"])
        t4 = sch.emit("scalar",
                      lambda sc: sc.activation(lnv[:, pp, :], mva[:, pp, 1:2],
                                               AF.Ln, bias=epsc[:, :], scale=1.0),
                      tk(t3, t_cst2, sm_rel.get(("lnv", pp))), ACs, 1,
                      reads=[f"amva:{pp}"], writes=[f"lnv:{pp}"])
        t5 = sch.emit("scalar",
                      lambda sc: sc.activation(inv[:, pp, :], lnv[:, pp, :],
                                               AF.Exp, bias=zc[:, :], scale=-0.5),
                      tk(t_cst3, sm_rel.get(("inv", pp))), ACs, 1,
                      reads=[f"lnv:{pp}"], writes=[f"inv:{pp}"])
        t6 = sch.emit("vector",
                      lambda v: v.tensor_scalar(xn_st[:, pp, :], x_sb[:, b, :],
                                                mva[:, pp, 0:1], inv[:, pp, :],
                                                op0=ALU.subtract, op1=ALU.mult),
                      tk(t5, sm_rel.get(("xn_st", pp))), DVs, 1,
                      reads=[xk + ":0", xk + ":1", f"mva:{pp}"], writes=[f"xn_st:{pp}"])
        sm_rel[("mva", pp)] = tk(t4, t6)
        sm_rel[("lnv", pp)] = tk(t5)
        sm_rel[("inv", pp)] = tk(t6)
        sm_rel[("mv6", pp)] = tk(t3)
        return t6

    def ln_transposes(b, t6):
        pp = b % 2
        evs = []
        for h4 in range(2):
            evs.append(transpose_burst(
                [xn_st[:, pp, 128 * k:128 * (k + 1)]
                 for k in range(4 * h4, 4 * h4 + 4)],
                xnT[:, 4 * h4:4 * h4 + 4, b, :], tk(t6),
                wkey=f"xnT:{b}:{h4}"))
        sm_rel[("xn_st", pp)] = tk(*evs, (PEs, PEs.count))
        return tk(*evs)

    def ln_block(b, xin_ticket, phase):
        return ln_transposes(b, ln_stats(b, xin_ticket))

    def router_block(b, xnT_t):
        pp = b % 2
        wl = tk(xnT_t, bank_rel[5], t_rw, t_rb, sm_rel.get(("logit", pp)))
        tm = None
        for k in range(KC):
            tm = sch.emit("tensor",
                          lambda t, kk=k: t.matmul(bank(5, 0, NE),
                                                   xnT[:, kk, b, :], rw_sb[:, kk, :],
                                                   start=(kk == 0), stop=(kk == KC - 1)),
                          wl if k == 0 else None, PEs if k == KC - 1 else None, 1)
        t1 = sch.emit("vector",
                      lambda v: v.tensor_tensor(logit[:, pp, :], bank(5, 0, NE),
                                                rb_sb[:, :], op=ALU.add),
                      tk(tm), DVs, 1, reads=["ps5", "rb_sb"], writes=[f"logit:{pp}"])
        bank_rel[5] = tk(t1)
        t2 = sch.emit("vector",
                      lambda v: v.reduce_max(mxb[:, pp, :], logit[:, pp, :], axis=AX.X),
                      tk(sm_rel.get(("mxb", pp))), DVs, 1,
                      reads=[f"logit:{pp}"], writes=[f"mxb:{pp}"])
        t3 = sch.emit("vector",
                      lambda v: v.reduce_max(nmx[:, pp, :], logit[:, pp, :], axis=AX.X,
                                             negate=True),
                      tk(sm_rel.get(("nmx", pp))), DVs, 1,
                      reads=[f"logit:{pp}"], writes=[f"nmx:{pp}"])
        t4 = sch.emit("scalar",
                      lambda sc: sc.activation(probs[:, pp, :], logit[:, pp, :],
                                               AF.Exp, bias=nmx[:, pp, :], scale=1.0,
                                               accum_out=sume[:, pp, :]),
                      tk(t1, t3, sm_rel.get(("probs", pp)), sm_rel.get(("sume", pp))),
                      ACs, 1, reads=[f"alogit:{pp}"], writes=[f"probs:{pp}", f"sume:{pp}"])
        t5 = sch.emit("vector",
                      lambda v: v.reciprocal(gate[:, pp, :], sume[:, pp, :]),
                      tk(t4, sm_rel.get(("gate", pp))), DVs, 1,
                      reads=[f"sume:{pp}"], writes=[f"gate:{pp}"])
        t6 = sch.emit("vector",
                      lambda v: v.tensor_scalar(probs[:, pp, :], logit[:, pp, :],
                                                mxb[:, pp, 0:1], None, op0=ALU.is_equal),
                      tk(t4), DVs, 1,
                      reads=[f"logit:{pp}", f"mxb:{pp}"], writes=[f"probs:{pp}"])
        t7 = sch.emit("vector",
                      lambda v: v.tensor_scalar(pcoef[:, b, :], probs[:, pp, :],
                                                gate[:, pp, :], 0.5,
                                                op0=ALU.mult, op1=ALU.mult),
                      tk(t6), DVs, 1,
                      reads=[f"probs:{pp}", f"gate:{pp}"], writes=[f"pc:{b}"])
        sm_rel[("logit", pp)] = tk(t4, t6)
        sm_rel[("mxb", pp)] = tk(t6)
        sm_rel[("nmx", pp)] = tk(t4)
        sm_rel[("sume", pp)] = tk(t5)
        sm_rel[("gate", pp)] = tk(t7)
        sm_rel[("probs", pp)] = tk(t7)
        t8 = sch.emit("vector",
                      lambda v: v.tensor_copy(pcb[:, b, :], pcoef[:, b, :]),
                      tk(t7, sm_rel.get(("pcb", b))), DVs, 1,
                      reads=[f"pc:{b}"], writes=[f"pcb:{b}"])
        bank_rel[5] = tk(t1)

        def emit_ct():
            bb2 = tr_rot[0] % 2
            tr_rot[0] += 1
            ptc = sch.emit("tensor",
                           lambda t, bb=bb2: t.transpose(
                               PTh[0:8, 1024 * bb:1024 * bb + 128],
                               pcb[:, b, :], ident[:, :]),
                           tk(t8, tbank_rel[bb2], t_ident), PEs, 1)
            if bb2 == 0:
                dtc = sch.emit("vector",
                               lambda v, bb=bb2: v.tensor_copy(
                                   cTt[:, b, :], PTh[0:8, 1024 * bb:1024 * bb + 128]),
                               tk(ptc), DVs, 1, reads=["pt"], writes=[f"cT:{b}"])
            else:
                dtc = sch.emit("scalar",
                               lambda sc, bb=bb2: sc.copy(
                                   cTt[:, b, :], PTh[0:8, 1024 * bb:1024 * bb + 128]),
                               tk(ptc), ACs, 1, reads=["apt"], writes=[f"acT:{b}"])
            tbank_rel[bb2] = tk(dtc)
            sm_rel[("pcb", b)] = tk((PEs, PEs.count))
            return tk(dtc)
        return tk(t7), emit_ct


    ffn_rot = [0]

    def ffn_pass(b, e, parity, first, xnT_t, pc_t, ct_t, wexp_t, weave=None):
        evs = []
        ct_ticket = None
        if first:
            ct_ticket = ct_t() if callable(ct_t) else ct_t
        for fh in range(2):
            bk = ffn_rot[0] % 4
            ffn_rot[0] += 1
            wl = tk(bank_rel[bk], xnT_t, wexp_t)
            tm = None
            for k in range(KC):
                tm = sch.emit("tensor",
                              lambda t, kk=k, bb=bk, ff=fh: t.matmul(
                                  bank(bb), xnT[:, kk, b, :],
                                  wexp[:, parity, kk, 512 * ff:512 * (ff + 1)],
                                  start=(kk == 0), stop=(kk == KC - 1)),
                              wl if k == 0 else None,
                              PEs if k == KC - 1 else None, 1)
            if bk < 1:
                ev = sch.emit("vector",
                              lambda v, bb=bk, ff=fh: v.scalar_tensor_tensor(
                                  x_sb[:, b, 512 * ff:512 * (ff + 1)],
                                  bank(bb), pcoef[:, b, e:e + 1],
                                  x_sb[:, b, 512 * ff:512 * (ff + 1)],
                                  op0=ALU.mult, op1=ALU.add),
                              tk(tm, pc_t), DVs, 1,
                              reads=[f"ps{bk}", f"pc:{b}", f"x:{b}:{fh}"],
                              writes=[f"x:{b}:{fh}"])
                bank_rel[bk] = tk(ev)
            else:
                sl = bk - 1
                ta = sch.emit("scalar",
                              lambda sc, bb=bk, ss=sl: sc.activation(
                                  ffn_tmp[:, ss, :], bank(bb), AF.Copy, bias=0.0,
                                  scale=pcoef[:, b, e:e + 1]),
                              tk(tm, pc_t, sm_rel.get(("ffn_tmp", sl))), ACs, 1,
                              reads=[f"aps{bk}"], writes=[f"ffntmp:{ss}" if False else f"ffntmp:{sl}"])
                bank_rel[bk] = tk(ta)
                ev = sch.emit("vector",
                              lambda v, ss=sl, ff=fh: v.tensor_tensor(
                                  x_sb[:, b, 512 * ff:512 * (ff + 1)],
                                  x_sb[:, b, 512 * ff:512 * (ff + 1)],
                                  ffn_tmp[:, ss, :], op=ALU.add),
                              tk(ta), DVs, 1,
                              reads=[f"x:{b}:{fh}", f"ftmp:{sl}"],
                              writes=[f"x:{b}:{fh}"])
                sm_rel[("ffn_tmp", sl)] = tk(ev)
            evs.append(ev)
            if weave is not None:
                weave(fh)
        if first:
            for fh in range(2):
                bk = 4 + fh
                wl = tk(bank_rel[bk], ct_ticket, t_eb)
                tb = sch.emit("tensor",
                              lambda t, ff=fh, bb=bk: t.matmul(
                                  bank(bb), cTt[:, b, :],
                                  eb_sb[:, 512 * ff:512 * (ff + 1)],
                                  start=True, stop=True),
                              wl, PEs, 1)
                ev = sch.emit("vector",
                              lambda v, ff=fh, bb=bk: v.tensor_tensor(
                                  x_sb[:, b, 512 * ff:512 * (ff + 1)],
                                  x_sb[:, b, 512 * ff:512 * (ff + 1)],
                                  bank(bb), op=ALU.add),
                              tk(tb), DVs, 1,
                              reads=[f"ps{bk}", f"x:{b}:{fh}"], writes=[f"x:{b}:{fh}"])
                bank_rel[bk] = tk(ev)
                evs.append(ev)
        return tk(*evs)

    def store_block(s, b, ticket):
        pp = b % 2
        tc = sch.emit("scalar",
                      lambda sc: sc.copy(out_st[:, pp, :], x_sb[:, b, :]),
                      tk(ticket, outst_rel[pp]), ACs, 1,
                      reads=[f"ax:{b}"], writes=[f"outst:{pp}"])
        blk0 = (s * n_blk + b) * 128
        st = dma_s(y_d[blk0:blk0 + 128, :], out_st[:, pp, :], tk(tc))
        outst_rel[pp] = st
        return st

    n_ffn_phases = (1 if "A" in phases else 0) + (1 if "C" in phases else 0)
    total_passes = n_seq * n_ffn_phases * NE
    loads_done = [0]
    load_t = [None] * (total_passes + 1)

    def ensure_loads(upto):
        while loads_done[0] <= min(upto, total_passes - 1):
            gp = loads_done[0]
            e = gp % NE
            par = gp % 2
            load_t[gp] = dma_g(wexp[:, par, :, :], ew_d[e],
                               tk(wexp_rel[par]))
            loads_done[0] += 1

    gpass = [0]
    ln_tr_acc = []

    x1_done = [None] * n_blk
    x2_done = [None] * n_blk
    last_phase = phases[-1]

    for s in range(n_seq):
        x_t = [dma_g(x_sb[:, b, :],
                     x_d[(s * n_blk + b) * 128:(s * n_blk + b) * 128 + 128, :],
                     tk(xslot_rel[b]))
               for b in range(n_blk)]
        if s == 0:
            load_wproj()
            if "B" in phases:
                build_qkv_cache()

        xnT_t = [None] * n_blk
        xn3_t = [None] * n_blk
        pc_t = [None] * n_blk
        ct_t = [None] * n_blk

        if "A" in phases:
            ln_a = [None] * n_blk
            for p in range(NE):
                gp = gpass[0]
                ensure_loads(gp + 1)
                par = gp % 2
                if p == 0:
                    ln_a[0] = ln_stats(0, tk(x_t[0]))
                    if n_blk > 1:
                        ln_a[1] = ln_stats(1, tk(x_t[1]))
                    xnT_t[0] = ln_transposes(0, ln_a[0])
                    pc_t[0], ct_t[0] = router_block(0, xnT_t[0])
                for b in range(n_blk):
                    weave = None
                    if p == 0 and b + 1 < n_blk:
                        def weave(fh, bb1=b + 1):
                            pp1 = bb1 % 2
                            ln_tr_acc.append(transpose_burst(
                                [xn_st[:, pp1, 128 * k:128 * (k + 1)]
                                 for k in range(4 * fh, 4 * fh + 4)],
                                xnT[:, 4 * fh:4 * fh + 4, bb1, :], tk(ln_a[bb1]),
                                wkey=f"xnT:{bb1}:{fh}"))
                            if fh == 1:
                                sm_rel[("xn_st", pp1)] = tk(*ln_tr_acc,
                                                            (PEs, PEs.count))
                                xnT_t[bb1] = tk(*ln_tr_acc)
                                ln_tr_acc.clear()
                                pc_t[bb1], ct_t[bb1] = router_block(bb1, xnT_t[bb1])
                                if bb1 + 1 < n_blk:
                                    ln_a[bb1 + 1] = ln_stats(bb1 + 1, tk(x_t[bb1 + 1]))
                    ev = ffn_pass(b, p, par, p == 0, xnT_t[b], pc_t[b], ct_t[b],
                                  load_t[gp], weave=weave)
                    if p == NE - 1:
                        x1_done[b] = ev
                        if last_phase == "A":
                            xslot_rel[b] = tk(store_block(s, b, ev))
                wexp_rel[par] = tk((PEs, PEs.count))
                gpass[0] += 1
        else:
            for b in range(n_blk):
                x1_done[b] = tk(x_t[b])

        if "B" in phases:
            qkv_evt = [None] * n_blk
            ln2_t = [None] * n_blk
            ln2_s = [None] * n_blk
            qload_i = [0]

            def qkv_group(g):
                blks = [2 * g, 2 * g + 1]
                for b in blks:
                    if ln2_s[b] is None:
                        ln2_s[b] = ln_stats(b, x1_done[b])
                for b in blks:
                    ln2_t[b] = ln_transposes(b, ln2_s[b])
                all_ev = []
                for jb in range(6):
                    sl = qload_i[0] % 2
                    qload_i[0] += 1
                    fbase = 4 * jb
                    wt = dma_s(wqkv_sb[:, sl, :, :, :],
                               wqc_d[:, fbase:fbase + 4, :, :],
                               tk(wq_rel[sl], qc_box[0]))
                    for fi in range(4):
                        f = fbase + fi
                        bk = f % 2
                        wl = tk(bank_rel[bk], wt, ln2_t[blks[0]], ln2_t[blks[1]])
                        tm = None
                        for k in range(KC):
                            tm = sch.emit("tensor",
                                          lambda t, kk=k, bb=bk, ss=sl, ii=fi: t.matmul(
                                              bank(bb, 0, 256),
                                              wqkv_sb[:, ss, ii, kk, :],
                                              xnT[:, kk, 2 * g:2 * g + 2, :],
                                              start=(kk == 0), stop=(kk == KC - 1)),
                                          wl if k == 0 else None,
                                          PEs if k == KC - 1 else None, 1)
                        wq_rel[sl] = tk(wq_rel[sl], tm)
                        if f < 8:
                            eng, sem = (("vector", DVs) if bk == 0
                                        else ("scalar", ACs))
                            sA = (2 * g) % R_QT
                            def _cp(d_, s_, dve=(bk == 0)):
                                def f_(e2, d=d_, ss=s_, dv_=dve):
                                    return (e2.tensor_copy(d, ss) if dv_
                                            else e2.copy(d, ss))
                                return f_
                            ev1 = sch.emit(eng, _cp(qT[:, sA:sA + 2, f, :],
                                                    bank(bk, 0, 256)),
                                           tk(tm, qt_rel[sA], qt_rel[(sA + 1) % R_QT]),
                                           sem, 1,
                                           reads=[f"ps{bk}"],
                                           writes=[f"qT:{2*g}", f"qT:{2*g+1}"])
                            bank_rel[bk] = tk(ev1)
                            all_ev += [ev1]
                        elif f < 16:
                            eng, sem = (("vector", DVs) if bk == 0
                                        else ("scalar", ACs))
                            sK = (2 * g) % R_KT
                            def _cp2(d_, s_, dve=(bk == 0)):
                                def f_(e2, d=d_, ss=s_, dv_=dve):
                                    return (e2.tensor_copy(d, ss) if dv_
                                            else e2.copy(d, ss))
                                return f_
                            ev1 = sch.emit(eng, _cp2(kT[:, sK:sK + 2, f - 8, :],
                                                     bank(bk, 0, 256)),
                                           tk(tm, kt_rel[sK], kt_rel[(sK + 1) % R_KT]),
                                           sem, 1,
                                           reads=[f"ps{bk}"],
                                           writes=[f"kT:{2*g}", f"kT:{2*g+1}"])
                            bank_rel[bk] = tk(ev1)
                            all_ev += [ev1]
                        else:
                            fv = f - 16
                            if bk == 0:
                                ev = sch.emit("vector",
                                              lambda v, bb=bk, ff=fv: v.tensor_copy(
                                                  vfeat[:, ff % 2, :], bank(bb, 0, 256)),
                                              tk(tm, sm_rel.get(("vfeat", fv % 2))), DVs, 1,
                                              reads=[f"ps{bk}"], writes=[f"vf:{fv%2}"])
                            else:
                                ev = sch.emit("scalar",
                                              lambda sc, bb=bk, ff=fv: sc.copy(
                                                  vfeat[:, ff % 2, :], bank(bb, 0, 256)),
                                              tk(tm, sm_rel.get(("vfeat", fv % 2))), ACs, 1,
                                              reads=[f"aps{bk}"], writes=[f"avf:{fv%2}"])
                            bank_rel[bk] = tk(ev)
                            evs_v = transpose_burst_multi(
                                [(vfeat[:, fv % 2, 128 * half:128 * (half + 1)],
                                  vtok[:, b2 % R_V, 2 * fv:2 * fv + 2, 0:64])
                                 for half, b2 in enumerate(blks)],
                                tk(ev, v_rel[blks[0] % R_V], v_rel[blks[1] % R_V],
                                   t_vones),
                                wkey=f"v:{2*g}")
                            all_ev += evs_v
                            sm_rel[("vfeat", fv % 2)] = tk((PEs, PEs.count))
                return tk(*all_ev)

            def attn_block(b, weave=None):
                if b_stop == 0:
                    qt_rel[b % R_QT] = tk((PEs, PEs.count))
                    if b >= 1:
                        kt_rel[(b - 1) % R_KT] = tk((PEs, PEs.count))
                        v_rel[(b - 1) % R_V] = tk((PEs, PEs.count))
                    if b == n_blk - 1:
                        kt_rel[b % R_KT] = tk((PEs, PEs.count))
                        v_rel[b % R_V] = tk((PEs, PEs.count))
                    return tk(qkv_evt[b], x1_done[b])
                chunks = [c for c in range(3) if 0 <= b + c - 1 < n_blk]
                nch = len(chunks)
                pp = b % 2
                exp_t = [None] * H
                eslot = [0] * H
                wl_qkv = tk(qkv_evt[b],
                            qkv_evt[b + 1] if b + 1 < n_blk else None,
                            qkv_evt[b - 1] if b - 1 >= 0 else None)
                x2e = []
                for half in range(2):
                    heads = range(8 * half, 8 * half + 8)
                    for pi in range(4):
                        gpair = 4 * half + pi
                        bkA = 4 if gpair % 2 == 0 else 2
                        pslot = (b * 8 + gpair) % (R_EKM // 2)
                        tmp_ = [None, None]
                        for sub in range(2):
                            h = 8 * half + 2 * pi + sub
                            eslot[h] = 2 * pslot + sub
                            fc = h // 2
                            bk = bkA + sub
                            part = slice(64 * sub, 64 * sub + 64)
                            wl = tk(bank_rel[bk], wl_qkv)
                            tm = None
                            for i, c in enumerate(chunks):
                                kb = b + c - 1
                                tm = sch.emit("tensor",
                                              lambda t, ii=i, hh=h, bb=bk, ps=part,
                                              kk=kb, ff=fc: t.matmul(
                                                  bank(bb, 128 * ii, 128),
                                                  kT[ps, kk % R_KT, ff, :],
                                                  qT[ps, b % R_QT, ff, :],
                                                  start=True, stop=True),
                                              wl if i == 0 else None,
                                              PEs if i == nch - 1 else None, 1)
                            tmp_[sub] = tm
                        te0 = sch.emit("scalar",
                                       lambda sc, bb=bkA, ss=2 * pslot, nn=nch: sc.activation(
                                           ekm[:, ss, 0:128 * nn], bank(bb, 0, 128 * nn),
                                           AF.Exp, bias=zc[:, :], scale=0.125),
                                       tk(tmp_[0], ekm_rel[2 * pslot], t_cst3), ACs, 1,
                                       reads=[f"aps{bkA}"], writes=[f"ekm:{2*pslot}"])
                        te = sch.emit("scalar",
                                      lambda sc, bb=bkA + 1, ss=2 * pslot + 1, nn=nch: sc.activation(
                                          ekm[:, ss, 0:128 * nn], bank(bb, 0, 128 * nn),
                                          AF.Exp, bias=zc[:, :], scale=0.125),
                                      tk(tmp_[1], ekm_rel[2 * pslot + 1], t_cst3), ACs, 1,
                                      reads=[f"aps{bkA+1}"], writes=[f"ekm:{2*pslot+1}"])
                        bank_rel[bkA] = tk(te0)
                        bank_rel[bkA + 1] = tk(te)
                        te_by_sub = [te0, te]
                        for sub in range(2):
                            h = 8 * half + 2 * pi + sub
                            ss2 = 2 * pslot + sub
                            te = te_by_sub[sub]
                            mt = [None] * nch
                            for i, c in enumerate(chunks):
                                if c == 1:
                                    continue
                                m = mlo if c == 0 else mhi
                                mt[i] = sch.emit(
                                    "vector",
                                    lambda v, s2=ss2, ii=i, mm=m: v.tensor_tensor(
                                        ekm[:, s2, 128 * ii:128 * (ii + 1)],
                                        ekm[:, s2, 128 * ii:128 * (ii + 1)],
                                        mm[:, :], op=ALU.mult),
                                    tk(te, t_mlo, t_mhi), DVs, 1,
                                    reads=[f"ekm:{ss2}"], writes=[f"ekm:{ss2}"])
                            exp_t[h] = (tk(te), [tk(te) if x is None else tk(x)
                                                 for x in mt])
                    if b_stop == 1:
                        continue
                    tmh = {}
                    for h in heads:
                        bk = 2 + ((h % 8) // 4)
                        off = 128 * (h % 4)
                        te_t, mt = exp_t[h]
                        order = sorted(range(nch), key=lambda i: chunks[i] != 1)
                        base = tk(bank_rel[bk], te_t,
                                  *[v_rel[(b + c - 1) % R_V] for c in chunks])
                        tm = None
                        for j, i in enumerate(order):
                            kb = b + chunks[i] - 1
                            wl = tk(base, mt[i]) if j == 0 else tk(mt[i])
                            tm = sch.emit("tensor",
                                          lambda t, hh=h, ii=i, bb=bk, oo=off,
                                          kk=kb, s0=(j == 0), s1=(j == nch - 1): t.matmul(
                                              bank(bb, oo, 65),
                                              ekm[:, eslot[hh], 128 * ii:128 * (ii + 1)],
                                              vtok[:, kk % R_V, hh, :],
                                              start=s0, stop=s1),
                                          wl,
                                          PEs if j == nch - 1 else None, 1)
                        tmh[h] = tm
                    for h in heads:
                        ekm_rel[eslot[h]] = tk(tmh[h])
                    if weave is not None:
                        weave(half)
                    rec_t = {}
                    for jb, bk in enumerate((2, 3)):
                        h0 = 8 * half + 4 * jb
                        wl = tk(*[tmh[h] for h in range(h0, h0 + 4)],
                                sm_rel.get(("invl", pp, jb)))
                        rec = sch.emit("vector",
                                       lambda v, bb=bk, hh=h0: v.reciprocal(
                                           invl[:, pp, hh:hh + 4],
                                           bass.AP(PSh, 512 * bb + 64, [[3072, 128], [128, 4]])),
                                       wl, DVs, 1,
                                       reads=[f"ps{bk}"], writes=[f"invl:{pp}:{jb}"])
                        rec_t[bk] = rec
                    evh = []
                    for h in heads:
                        bk = 2 + ((h % 8) // 4)
                        off = 128 * (h % 4)
                        rt = rec_t[bk]
                        if bk == 2:
                            ev = sch.emit("vector",
                                          lambda v, hh=h, bb=bk, oo=off: v.tensor_scalar(
                                              anorm[:, pp, 64 * hh:64 * (hh + 1)],
                                              bank(bb, oo, 64),
                                              invl[:, pp, hh:hh + 1], None, op0=ALU.mult),
                                          tk(rt, sm_rel.get(("anorm", pp))), DVs, 1,
                                          reads=[f"ps{bk}", f"invl:{pp}:0"],
                                          writes=[f"anorm:{pp}:{h}"])
                        else:
                            ev = sch.emit("scalar",
                                          lambda sc, hh=h, bb=bk, oo=off: sc.activation(
                                              anorm[:, pp, 64 * hh:64 * (hh + 1)],
                                              bank(bb, oo, 64), AF.Copy, bias=0.0,
                                              scale=invl[:, pp, hh:hh + 1]),
                                          tk(rt, sm_rel.get(("anorm", pp))), ACs, 1,
                                          reads=[f"aps{bk}", f"ainvl:{pp}"],
                                          writes=[f"anorm:{pp}:{h}"])
                        evh.append(ev)
                        bank_rel[bk] = tk(bank_rel[bk], ev)
                    x2e += evh
                if b_stop == 1:
                    ee = tk(*[exp_t[h] for h in range(H)], x1_done[b], qkv_evt[b])
                    qt_rel[b % R_QT] = tk((PEs, PEs.count))
                    if b >= 1:
                        kt_rel[(b - 1) % R_KT] = tk((PEs, PEs.count))
                        v_rel[(b - 1) % R_V] = tk((PEs, PEs.count))
                    if b == n_blk - 1:
                        kt_rel[b % R_KT] = tk((PEs, PEs.count))
                        v_rel[b % R_V] = tk((PEs, PEs.count))
                    for h in range(H):
                        ekm_rel[eslot[h]] = ee
                    return ee
                sm_rel[("invl", pp, 0)] = tk(*x2e)
                sm_rel[("invl", pp, 1)] = tk(*x2e)
                if b_stop == 2:
                    ee = tk(*x2e, x1_done[b])
                    qt_rel[b % R_QT] = tk((PEs, PEs.count))
                    if b >= 1:
                        kt_rel[(b - 1) % R_KT] = tk((PEs, PEs.count))
                        v_rel[(b - 1) % R_V] = tk((PEs, PEs.count))
                    if b == n_blk - 1:
                        kt_rel[b % R_KT] = tk((PEs, PEs.count))
                        v_rel[b % R_V] = tk((PEs, PEs.count))
                    return ee
                aT_l = []
                for h4 in range(2):
                    aT_l.append(transpose_burst(
                        [anorm[:, pp, 128 * k:128 * (k + 1)]
                         for k in range(4 * h4, 4 * h4 + 4)],
                        aT[:, pp, 4 * h4:4 * h4 + 4, :], tk(*x2e),
                        wkey=f"aT:{pp}:{h4}"))
                aT_t = tk(*aT_l)
                sm_rel[("anorm", pp)] = tk(aT_t, (PEs, PEs.count))
                pevs = []
                for fh in range(2):
                    bk = fh
                    wl = tk(bank_rel[bk], aT_t, t_wproj_box[0], sm_rel.get(("aTr", pp)))
                    tm = None
                    for k in range(KC):
                        tm = sch.emit("tensor",
                                      lambda t, kk=k, bb=bk, ff=fh: t.matmul(
                                          bank(bb), aT[:, pp, kk, :],
                                          wproj_sb[:, kk, 512 * ff:512 * (ff + 1)],
                                          start=(kk == 0), stop=(kk == KC - 1)),
                                      wl if k == 0 else None,
                                      PEs if k == KC - 1 else None, 1)
                    ev = sch.emit("vector",
                                  lambda v, bb=bk, ff=fh: v.tensor_tensor(
                                      x_sb[:, b, 512 * ff:512 * (ff + 1)],
                                      x_sb[:, b, 512 * ff:512 * (ff + 1)],
                                      bank(bb), op=ALU.add),
                                  tk(tm), DVs, 1,
                                  reads=[f"ps{bk}", f"x:{b}:{fh}"], writes=[f"x:{b}:{fh}"])
                    bank_rel[bk] = tk(ev)
                    pevs.append(ev)
                sm_rel[("aTr", pp)] = tk((PEs, PEs.count))
                tbp = sch.emit("vector",
                               lambda v: v.tensor_tensor(x_sb[:, b, :], x_sb[:, b, :],
                                                         bproj_sb[:, :], op=ALU.add),
                               tk(t_bp, *pevs), DVs, 1,
                               reads=["bproj_sb", f"x:{b}:0", f"x:{b}:1"], writes=[f"x:{b}:0", f"x:{b}:1"])
                qt_rel[b % R_QT] = tk(tbp, (PEs, PEs.count))
                if b >= 1:
                    kt_rel[(b - 1) % R_KT] = tk((PEs, PEs.count))
                    v_rel[(b - 1) % R_V] = tk((PEs, PEs.count))
                if b == n_blk - 1:
                    kt_rel[b % R_KT] = tk((PEs, PEs.count))
                    v_rel[b % R_V] = tk((PEs, PEs.count))
                return tk(tbp)

            attn_next = 0
            for g in range(n_blk // 2):
                gt = qkv_group(g)
                qkv_evt[2 * g] = gt
                qkv_evt[2 * g + 1] = gt
                while attn_next < n_blk and (
                        attn_next + 1 >= n_blk or qkv_evt[attn_next + 1] is not None):
                    bb_ = attn_next
                    weave = None
                    if last_phase != "B" and "C" in phases and bb_ >= 1:
                        bprev = bb_ - 1
                        st3 = ln_stats(bprev, x2_done[bprev])
                        tr3 = []

                        def weave(half, bp=bprev, st=st3, trl=tr3):
                            pp3 = bp % 2
                            trl.append(transpose_burst(
                                [xn_st[:, pp3, 128 * k:128 * (k + 1)]
                                 for k in range(4 * half, 4 * half + 4)],
                                xnT[:, 4 * half:4 * half + 4, bp, :], tk(st),
                                wkey=f"xnT:{bp}:{half}"))
                            if half == 1:
                                sm_rel[("xn_st", pp3)] = tk(*trl, (PEs, PEs.count))
                                xn3_t[bp] = tk(*trl)
                                pc_t[bp], ct_t[bp] = router_block(bp, xn3_t[bp])
                    x2_done[bb_] = attn_block(bb_, weave=weave)
                    if last_phase == "B":
                        xslot_rel[bb_] = tk(store_block(s, bb_, x2_done[bb_]))
                    attn_next += 1
            if last_phase != "B" and "C" in phases and n_blk >= 1:
                bl = n_blk - 1
                xn3_t[bl] = ln_block(bl, x2_done[bl], "C")
                pc_t[bl], ct_t[bl] = router_block(bl, xn3_t[bl])
        else:
            for b in range(n_blk):
                x2_done[b] = x1_done[b]

        if "C" in phases:
            p_start = DBG_P_START if ("B" in phases and last_phase != "B" and not DBG_NO_C0) else 0
            for p in range(p_start, NE):
                gp = gpass[0]
                ensure_loads(gp + 1)
                par = gp % 2
                for b in range(n_blk):
                    if p == p_start and xn3_t[b] is None:
                        xn3_t[b] = ln_block(b, x2_done[b], "C")
                        pc_t[b], ct_t[b] = router_block(b, xn3_t[b])
                    ev = ffn_pass(b, p, par, (p == 0) or (p == 1 and not DBG_C0_BIAS
                                               and p_start == 1), xn3_t[b], pc_t[b],
                                  ct_t[b], load_t[gp])
                    if p == NE - 1:
                        xslot_rel[b] = tk(store_block(s, b, ev))
                wexp_rel[par] = tk((PEs, PEs.count))
                gpass[0] += 1

        if s == 0:
            for nm, shp in taps:
                assert shp == [n_blk * 128, DM]
                for b in range(n_blk):
                    dma_g(tap_d[nm][b * 128:(b + 1) * 128, :], x_sb[:, b, :],
                          tk(xslot_rel[b] if last_phase == "C" else x2_done[b]))

    if DBG_DUMP_PC:
        dbg_pc = nc.dram_tensor("dbg_pc", [n_blk * 128, NE], F32, kind="ExternalOutput").ap()
        dbg_ct = nc.dram_tensor("dbg_ct", [NE * n_blk, 128], F32, kind="ExternalOutput").ap()
        for b in range(n_blk):
            dma_g(dbg_pc[b * 128:(b + 1) * 128, :], pcoef[:, b, :],
                  tk((PEs, PEs.count), (DVs, DVs.count)))
        dma_g(dbg_ct.rearrange("(e b) f -> e b f", e=NE), cTt[:, :, :],
              tk((PEs, PEs.count), (DVs, DVs.count), (ACs, ACs.count)))
        dbg_x = nc.dram_tensor("dbg_x", [n_blk * 128, DM], F32, kind="ExternalOutput").ap()
        for b in range(n_blk):
            dma_g(dbg_x[b * 128:(b + 1) * 128, :], x_sb[:, b, :],
                  tk((PEs, PEs.count), (DVs, DVs.count), (ACs, ACs.count)))
        dbg_xnt = nc.dram_tensor("dbg_xnt", [128, KC, n_blk, 128], F32, kind="ExternalOutput").ap()
        dma_g(dbg_xnt, xnT[:, :, :, :], tk((PEs, PEs.count), (DVs, DVs.count), (ACs, ACs.count)))

    sch.emit("sync", lambda g: g.nop(),
             tk(*[(l, l.count) for l in DS_l], *[(l, l.count) for l in DG_l]), None)

    for sem in sch.sems.values():
        cm = nc.semaphore(sem.name)
        sem.handle = cm.__enter__()
        ctx.append(cm)
    with nc.Block() as block:
        block.tensor(lambda t: sch.replay_engine("tensor", t))
        block.vector(lambda v: sch.replay_engine("vector", v))
        block.scalar(lambda sc: sch.replay_engine("scalar", sc))
        block.gpsimd(lambda g: sch.replay_engine("gpsimd", g))
        block.sync(lambda sy: sch.replay_engine("sync", sy))

    for t in reversed(ctx):
        t.__exit__(None, None, None)
    assert sbuf_bytes[0] <= 207 * 1024, f"SBUF over budget: {sbuf_bytes[0]}"
    build_stats = dict(steps=len(sch.steps), drains=sch.n_drains,
                       sbuf=sbuf_bytes[0])
    return nc, build_stats


def _pm_kf(w):
    f = w.shape[1]
    return np.ascontiguousarray(w.reshape(KC, 128, f).transpose(1, 0, 2))


def _pm_qkv(w):
    return np.ascontiguousarray(
        w.reshape(KC, 128, 24, 128).transpose(1, 2, 0, 3))


def _pm_ew(w):
    return np.ascontiguousarray(
        w.reshape(NE, KC, 128, DM).transpose(0, 2, 1, 3).reshape(NE, 128, KC * DM))


def _consts():
    import ml_dtypes
    bfd = ml_dtypes.bfloat16
    ident = np.eye(128, dtype=np.float32).astype(bfd)
    r = np.arange(128)
    mlo = (r[:, None] >= r[None, :]).astype(np.float32).astype(bfd)
    mhi = (r[:, None] <= r[None, :]).astype(np.float32).astype(bfd)
    return ident, mlo, mhi


def make_in_map(x_shard, w):
    ident, mlo, mhi = _consts()
    return {
        "x": np.ascontiguousarray(x_shard.reshape(-1, DM)),
        "wqkv": w["wqkv"], "wproj": w["wproj"], "bproj": w["bproj"],
        "rw": w["rw"], "rb": w["rb"], "ew": w["ew"], "eb": w["eb"],
        "identd": ident, "mlod": mlo, "mhid": mhi,
    }


def kernel(L0_attn_x, L0_attn_w_qkv, L0_attn_w_proj, L0_attn_b_proj,
           L0_ffn_router_w, L0_ffn_router_b, L0_ffn_expert_w, L0_ffn_expert_b):
    x = np.asarray(L0_attn_x, dtype=np.float32)
    w = {
        "wqkv": _pm_qkv(np.asarray(L0_attn_w_qkv, dtype=np.float32)),
        "wproj": _pm_kf(np.asarray(L0_attn_w_proj, dtype=np.float32)),
        "bproj": np.ascontiguousarray(L0_attn_b_proj, dtype=np.float32).reshape(1, DM),
        "rw": _pm_kf(np.asarray(L0_ffn_router_w, dtype=np.float32)),
        "rb": np.ascontiguousarray(L0_ffn_router_b, dtype=np.float32).reshape(1, NE),
        "ew": _pm_ew(np.asarray(L0_ffn_expert_w, dtype=np.float32)),
        "eb": np.ascontiguousarray(L0_ffn_expert_b, dtype=np.float32),
    }
    nc, _ = build_nc()
    in_maps = [make_in_map(x[SEQ_PER_CORE * c:SEQ_PER_CORE * (c + 1)], w)
               for c in range(NCORE)]
    res = run_bass_kernel_spmd(nc, in_maps, core_ids=list(range(NCORE)))
    out = np.concatenate(
        [np.asarray(r["y"]).reshape(SEQ_PER_CORE, S, DM) for r in res.results],
        axis=0)
    return out.astype(np.float32)
